# version 22
# baseline (speedup 1.0000x reference)
import numpy as np
import concourse.bass as bass
import concourse.mybir as mybir
from concourse.bass_utils import run_bass_kernel_spmd

F32 = mybir.dt.float32
BF16 = mybir.dt.bfloat16
I32 = mybir.dt.int32
U8 = mybir.dt.uint8
AF = mybir.ActivationFunctionType
ALU = mybir.AluOpType

NCORE = 8
DM = 1024
DI = 1024
NST = 16
SEQ = 16384
TOK = SEQ // NCORE
HALO = 3
NCH = DM // 128
TC = 512
NCHUNK = TOK // TC
HID = 4096
NEXP = 16
TOPK = 4
EPS = 1e-5
BIG = 1e30

ENGINES = ("sync", "vector", "scalar", "tensor", "gpsimd")


class Plan:
    def __init__(self, nc, stack):
        self.nc = nc
        self.stack = stack
        self.ops = {e: [] for e in ENGINES}
        self.cnt = {}
        self.sems = {}

    def sem(self, name):
        if name not in self.sems:
            self.sems[name] = self.stack.enter_context(self.nc.semaphore(name))
            self.cnt[name] = 0
        return self.sems[name]

    def op(self, eng, fn, inc=None, inc_n=1):
        if inc is not None:
            s = self.sem(inc)
            self.cnt[inc] += inc_n
            n = inc_n

            def wrapped(e, fn=fn, s=s, n=n):
                fn(e).then_inc(s, n)

            self.ops[eng].append(wrapped)
        else:
            self.ops[eng].append(lambda e, fn=fn: fn(e))

    def wait(self, eng, sem, val=None):
        self.sem(sem)
        v = self.cnt[sem] if val is None else val
        if v <= 0:
            return
        s = self.sems[sem]
        self.ops[eng].append(lambda e, s=s, v=v: e.wait_ge(s, v))

    def val(self, sem):
        self.sem(sem)
        return self.cnt[sem]


class Carver:
    def __init__(self, nc, total_bytes):
        self.t = nc.alloc_sbuf_tensor("arena", [128, total_bytes], U8)
        self.off = 0
        self.marks = []

    def view(self, shape, dtype):
        es = 4 if dtype in (F32, I32) else 2
        n = shape[1] * es
        o = self.off
        self.off = (self.off + n + 63) & ~63
        return self.t[0:shape[0], o:o + n].bitcast(dtype)

    def push(self):
        self.marks.append(self.off)

    def pop(self):
        self.off = self.marks.pop()


def build_core(debug=0):
    import contextlib
    nc = bass.Bass("TRN2", target_bir_lowering=False)
    stack = contextlib.ExitStack()

    def din(name, shape, dt=F32):
        return nc.dram_tensor(name, shape, dt, kind="ExternalInput").ap()

    x_in = din("x", [DM, HALO + TOK])
    winT = din("winT", [DM, 2 * DI])
    wdtT = din("wdtT", [DI, DI])
    woutT = din("woutT", [DI, DM])
    w1T = din("w1T", [DM, HID])
    w2T = din("w2T", [HID, DM])
    routT = din("routT", [DM, NEXP])
    rout_b = din("rout_b", [1, NEXP])
    conv_w = din("conv_w", [DI, 4])
    conv_b = din("conv_b", [DI, 1])
    b_dt = din("b_dt", [DI, 1])
    a_log = din("a_log", [DI, NST])
    bp_in = din("bp", [DI, NST])
    cp_in = din("cp", [DI, NST])
    dp_in = din("dp", [DI, 1])
    b1c = din("b1c", [128, HID // 128])
    b2c = din("b2c", [128, DM // 128])

    out_d = nc.dram_tensor("out", [DM, TOK], F32, kind="ExternalOutput").ap()
    okind = "ExternalOutput" if debug else "Internal"
    delta_d = nc.dram_tensor("delta_d", [DI, TOK], BF16, kind=okind).ap()
    g1_d = nc.dram_tensor("g1_d", [DI, TOK], BF16, kind=okind).ap()
    g2_d = nc.dram_tensor("g2_d", [DI, TOK], BF16, kind=okind).ap()
    w1b_d = nc.dram_tensor("w1b_d", [DM, HID], BF16).ap()
    w2b_d = nc.dram_tensor("w2b_d", [HID, DM], BF16).ap()
    cin_d = nc.dram_tensor("cin_d", [DI + 128, NST], F32).ap()
    me_d = [nc.dram_tensor(f"me_d{i}", [NEXP, TC], BF16).ap() for i in range(2)]
    w_d = nc.dram_tensor("w_d", [DI, TOK], BF16).ap()
    cout_d = nc.dram_tensor("cout_d", [NCORE * (DI + 128), NST], F32).ap()
    if debug:
        dbg_y = nc.dram_tensor("dbg_y", [DI, TOK], BF16, kind="ExternalOutput").ap()
        dbg_x1 = nc.dram_tensor("dbg_x1", [DM, TOK], BF16, kind="ExternalOutput").ap()
        dbg_b = nc.dram_tensor("dbg_b", [DI, NST + 1], F32, kind="ExternalOutput").ap()
        dbg_hs = nc.dram_tensor("dbg_hs", [DI, NST], F32, kind="ExternalOutput").ap()
        dbg_hsall = nc.dram_tensor("dbg_hsall", [128, NCORE * NST], F32, kind="ExternalOutput").ap()
        dbg_bdj = nc.dram_tensor("dbg_bdj", [128, NCH * (NST + 1)], F32, kind="ExternalOutput").ap()
        dbg_msk = nc.dram_tensor("dbg_msk", [TOK, NEXP], F32, kind="ExternalOutput").ap()
        dbg_lg = nc.dram_tensor("dbg_lg", [TOK, NEXP], F32, kind="ExternalOutput").ap()
        dbg_xn2 = nc.dram_tensor("dbg_xn2", [DM, TOK], BF16, kind="ExternalOutput").ap()

    cv = Carver(nc, 207 * 1024)
    convw_t = [cv.view([128, 4], F32) for _ in range(NCH)]
    convb_t = [cv.view([128, 1], F32) for _ in range(NCH)]
    bdt_t = [cv.view([128, 1], F32) for _ in range(NCH)]
    alog_t = [cv.view([128, NST], F32) for _ in range(NCH)]
    pexp_t = [cv.view([128, NST], F32) for _ in range(NCH)]
    negp_t = [cv.view([128, NST], F32) for _ in range(NCH)]
    bc_t = [cv.view([128, NST], F32) for _ in range(NCH)]
    dp_t = [cv.view([128, 1], F32) for _ in range(NCH)]
    b1_t = cv.view([128, HID // 128], F32)
    b2_t = cv.view([128, DM // 128], F32)
    rb_t = cv.view([128, NEXP], F32)
    rout_sb = [cv.view([128, NEXP], BF16) for _ in range(NCH)]
    ones_k = cv.view([128, 1], BF16)
    ones_kf = cv.view([128, 1], F32)
    ones_row = cv.view([1, 128], F32)
    ones_rowb = cv.view([1, 128], BF16)
    ident = cv.view([128, 128], F32)
    iotaA = cv.view([128, 128], I32)
    iotaB = cv.view([128, 128], I32)
    zero512 = cv.view([128, TC], F32)
    btil = [cv.view([128, NST], F32) for _ in range(NCH)]
    bccum = cv.view([128, NST], F32)
    pc_t = [cv.view([128, NST], F32) for _ in range(NCH)]
    dtot = [cv.view([128, 1], F32) for _ in range(NCH)]
    state = [cv.view([128, NST], F32) for _ in range(NCH)]
    hsall = [cv.view([128, NCORE * NST], F32) for _ in range(NCH)]
    bdj2 = [cv.view([128, NCH * NST], F32) for _ in range(2)]
    bdjD2 = [cv.view([128, NCH], F32) for _ in range(2)]
    dtotall = cv.view([128, NST], F32)
    tails = [cv.view([128, HALO], F32) for _ in range(NCH)]
    tiny = cv.view([128, 1], F32)

    cv.push()
    warena = cv.off
    win_sb = [cv.view([128, 2 * DI], BF16) for _ in range(NCH)]
    wdt_sb = [cv.view([128, DI], BF16) for _ in range(NCH)]
    p1w_end = cv.off
    cv.pop()
    cv.off = warena
    wout_sb = [cv.view([128, DM], BF16) for _ in range(NCH)]
    w1q_sb = [[cv.view([128, HID // 4], BF16) for _ in range(NCH)] for _ in range(2)]
    w2h_sb = [cv.view([128, DM], BF16) for _ in range(8)]
    cv.off = max(cv.off, p1w_end)

    cv.push()
    tarena = cv.off
    xbf8_t = [cv.view([128, TC], BF16) for _ in range(NCH)]
    xsqb_t = [cv.view([128, TC], BF16) for _ in range(2)]
    xln_t = [cv.view([128, TC], BF16) for _ in range(NCH)]
    xp515 = [cv.view([128, HALO + TC], F32) for _ in range(2)]
    cacc_t = [cv.view([128, TC], F32) for _ in range(2)]
    xc_t = [cv.view([128, TC], F32) for _ in range(2)]
    xcb_t = [cv.view([128, TC], BF16) for _ in range(NCH)]
    srb_t = [cv.view([128, TC], BF16) for _ in range(4)]
    g1s_t = [cv.view([128, TC], BF16) for _ in range(4)]
    sp_t = [cv.view([128, TC], F32) for _ in range(4)]
    dlt_t = [cv.view([128, TC], F32) for _ in range(2)]
    cum_t = [cv.view([128, TC], F32) for _ in range(2)]
    sfx_t = [cv.view([128, TC], F32) for _ in range(2)]
    bexp_t = [cv.view([128, TC], BF16) for _ in range(6)]
    wp_t = [cv.view([128, TC], BF16) for _ in range(2)]
    p1_end = cv.off
    cv.pop()
    dltb_t = [cv.view([128, TC], BF16) for _ in range(2)]
    wr_t = [cv.view([128, TC], BF16) for _ in range(2)]
    da_t = [cv.view([128, TC], BF16) for _ in range(8)]
    gbuf = cv.view([128, NST * TC], BF16)
    y_t = cv.view([128, TC], F32)
    gated_t = [[cv.view([128, TC], BF16) for _ in range(NCH)] for _ in range(2)]
    g1r_t = [cv.view([128, TC], BF16) for _ in range(3)]
    g2r_t = [cv.view([128, TC], BF16) for _ in range(3)]
    x1_t = [[cv.view([128, TC], BF16) for _ in range(NCH)] for _ in range(2)]
    x1sq_t = [cv.view([128, TC], BF16) for _ in range(2)]
    xn2_t = [[cv.view([128, TC], BF16) for _ in range(NCH)] for _ in range(2)]
    xr_t = [cv.view([128, TC], F32) for _ in range(2)]
    z_t = [cv.view([128, TC], BF16) for _ in range(8)]
    ysb_t = [cv.view([128, TC], BF16) for _ in range(NCH)]
    yout_t = [cv.view([128, TC], F32) for _ in range(2)]
    lg_t = [cv.view([128, NEXP], F32) for _ in range(4)]
    lgw_t = [cv.view([128, NEXP], F32) for _ in range(4)]
    mx_t = cv.view([128, 1], F32)
    m4_t = [cv.view([128, 1], F32) for _ in range(4)]
    msk_t = [cv.view([128, NEXP], F32) for _ in range(4)]
    me_t = cv.view([NEXP, TC], BF16)
    me_row4 = cv.view([128, 4 * TC], BF16)
    rbst = cv.view([1, NEXP], F32)
    cv.off = max(cv.off, p1_end)
    rA = cv.view([1, TC], F32)
    rB = cv.view([1, TC], F32)
    rC = cv.view([1, TC], F32)
    assert cv.off <= 207 * 1024, f"sbuf overflow {cv.off}"

    psA = nc.alloc_psum_tensor("psA", [128, TC], F32)[:, :]
    psB = nc.alloc_psum_tensor("psB", [128, TC], F32)[:, :]
    psC = nc.alloc_psum_tensor("psC", [128, TC], F32)[:, :]
    psD = nc.alloc_psum_tensor("psD", [128, TC], F32)[:, :]
    psE = nc.alloc_psum_tensor("psE", [128, TC], F32)[:, :]
    psF = nc.alloc_psum_tensor("psF", [128, TC], F32)[:, :]
    PS = {"A": psA, "B": psB, "C": psC, "D": psD, "E": psE, "F": psF}

    c_eps = cv.view([128, 1], F32)
    c_one = cv.view([128, 1], F32)
    c_neg1 = cv.view([128, 1], F32)
    nc.const_aps.register(F32, EPS, c_eps)
    nc.const_aps.register(F32, 1.0, c_one)
    nc.const_aps.register(F32, -1.0, c_neg1)

    pl = Plan(nc, stack)
    bank_free = {}

    def pe_guard(bank):
        b = bank_free.get(bank)
        if b is not None:
            sem, cnt = b
            pl.wait("tensor", sem, cnt)

    def bank_done(bank, sem):
        bank_free[bank] = (sem, pl.val(sem))

    for k in range(NCH):
        r = slice(k * 128, (k + 1) * 128)
        for dst, src in ((convw_t[k], conv_w[r, :]), (convb_t[k], conv_b[r, :]),
                         (bdt_t[k], b_dt[r, :]), (alog_t[k], a_log[r, :]),
                         (dp_t[k], dp_in[r, :]), (btil[k], bp_in[r, :]),
                         (pc_t[k], cp_in[r, :])):
            pl.op("sync", lambda e, d=dst, s=src: e.dma_start(out=d, in_=s), inc="sX", inc_n=16)
    pl.op("sync", lambda e: e.dma_start(out=b1_t, in_=b1c[:, :]), inc="sX", inc_n=16)
    pl.op("sync", lambda e: e.dma_start(out=b2_t, in_=b2c[:, :]), inc="sX", inc_n=16)
    pl.op("sync", lambda e: e.dma_start(out=rbst, in_=rout_b), inc="sX", inc_n=16)
    pl.wait("vector", "sX")
    for k in range(NCH):
        pl.op("vector", lambda e, k=k: e.tensor_mul(bc_t[k], btil[k], pc_t[k]), inc="sDVE")
    pl.op("vector", lambda e: e.memset(ones_k, 1.0), inc="sDVE")
    pl.op("vector", lambda e: e.memset(ones_kf, 1.0), inc="sDVE")
    pl.op("vector", lambda e: e.memset(ones_row, 1.0), inc="sDVE")
    pl.op("vector", lambda e: e.memset(ones_rowb, 1.0), inc="sDVE")
    pl.op("vector", lambda e: e.memset(zero512, 0.0), inc="sDVE")
    pl.op("vector", lambda e: e.memset(c_eps, EPS), inc="sDVE")
    pl.op("vector", lambda e: e.memset(c_one, 1.0), inc="sDVE")
    pl.op("vector", lambda e: e.memset(c_neg1, -1.0), inc="sDVE")
    pl.wait("vector", "sDVE")
    for k in range(NCH):
        pl.op("vector", lambda e, k=k: e.memset(btil[k], 0.0), inc="sDVE")
        pl.op("vector", lambda e, k=k: e.memset(dtot[k], 0.0), inc="sDVE")
        pl.op("vector", lambda e, k=k: e.memset(tails[k], 0.0), inc="sDVE")
    pl.op("gpsimd", lambda e: e.iota(iotaA, [[0, 128]], channel_multiplier=1), inc="sPL")
    pl.op("gpsimd", lambda e: e.iota(iotaB, [[1, 128]], channel_multiplier=0), inc="sPL")
    pl.wait("vector", "sPL")
    pl.op("vector", lambda e: e.tensor_tensor(ident, iotaA, iotaB, ALU.is_equal), inc="sDVE")
    pl.wait("scalar", "sX")
    for k in range(NCH):
        pl.op("scalar", lambda e, k=k: e.activation(pexp_t[k], alog_t[k], AF.Exp), inc="sACT")
    pl.wait("vector", "sACT")
    for k in range(NCH):
        pl.op("vector", lambda e, k=k: e.tensor_scalar(negp_t[k], pexp_t[k], -1.0, None, ALU.mult), inc="sDVE")
    consts_dve = pl.val("sDVE")
    pl.wait("tensor", "sX")
    pl.wait("tensor", "sDVE", consts_dve)
    pl.op("tensor", lambda e: e.matmul(psF[:, 0:NEXP], ones_row, rbst, start=True, stop=True), inc="sPE")
    pl.wait("vector", "sPE")
    pl.op("vector", lambda e: e.tensor_copy(rb_t, psF[:, 0:NEXP]), inc="sDVE")
    bank_done("F", "sDVE")
    srcs = [w1T[i * 128:(i + 1) * 128, h * 2048:(h + 1) * 2048] for i in range(8) for h in range(2)] + \
           [w2T[i * 128:(i + 1) * 128, :] for i in range(32)]
    dsts = [w1b_d[i * 128:(i + 1) * 128, h * 2048:(h + 1) * 2048] for i in range(8) for h in range(2)] + \
           [w2b_d[i * 128:(i + 1) * 128, :] for i in range(32)]
    precast_jobs = list(zip(srcs, dsts))

    def emit_precast(n):
        for _ in range(min(n, len(precast_jobs))):
            s, d = precast_jobs.pop(0)
            pl.op("gpsimd", lambda e, s=s, d=d: e.dma_start(out=d, in_=s), inc="sWC", inc_n=16)

    norm_hist = [None]

    def ln_block(xsrc, width, xlnout, loadw=None):
        loadw = loadw or width
        prev = norm_hist[0]
        for k in range(NCH):
            if prev is not None:
                pl.wait("gpsimd", "sDVE", prev[k])
            pl.op("gpsimd", lambda e, k=k: e.dma_start(out=xbf8_t[k][:, 0:loadw], in_=xsrc(k)), inc="sXB", inc_n=16)
        xw = pl.val("sXB")
        for k in range(NCH):
            pl.wait("scalar", "sXB", xw)
            pl.wait("scalar", "sPE")
            pl.op("scalar", lambda e, k=k: e.activation(xsqb_t[k % 2][:, 0:width], xbf8_t[k][:, 0:width], AF.Square), inc="sACT")
            pl.wait("tensor", "sACT")
            pl.wait("tensor", "sDVE", consts_dve)
            if k < 2:
                pe_guard("E")
                pe_guard("F")
            pl.op("tensor", lambda e, k=k: e.matmul(psE[0:1, 0:width], ones_k, xbf8_t[k][:, 0:width],
                                                    start=(k == 0), stop=(k == NCH - 1)), inc="sPE")
            pl.op("tensor", lambda e, k=k: e.matmul(psF[32:33, 0:width], ones_k, xsqb_t[k % 2][:, 0:width],
                                                    start=(k == 0), stop=(k == NCH - 1)), inc="sPE")
        pl.wait("vector", "sPE")
        pl.wait("vector", "sPE2") if False else None
        pl.op("vector", lambda e: e.tensor_scalar(rA[:, 0:width], psE[0:1, 0:width], 1.0 / DM, None, ALU.mult), inc="sDVE")
        pl.op("vector", lambda e: e.tensor_scalar(rB[:, 0:width], psF[32:33, 0:width], 1.0 / DM, None, ALU.mult), inc="sDVE")
        bank_done("E", "sDVE")
        bank_done("F", "sDVE")
        pl.wait("vector", "sDVE")
        pl.op("vector", lambda e: e.tensor_mul(rC[:, 0:width], rA[:, 0:width], rA[:, 0:width]), inc="sDVE")
        pl.wait("vector", "sDVE")
        pl.op("vector", lambda e: e.tensor_tensor(rB[:, 0:width], rB[:, 0:width], rC[:, 0:width], ALU.subtract), inc="sDVE")
        pl.wait("scalar", "sDVE")
        pl.op("scalar", lambda e: e.activation(rB[:, 0:width], rB[:, 0:width], AF.Sqrt, bias=EPS), inc="sACT")
        pl.wait("vector", "sACT")
        pl.op("vector", lambda e: e.reciprocal(rC[:, 0:width], rB[:, 0:width]), inc="sDVE")
        pl.wait("tensor", "sDVE")
        pe_guard("E")
        pe_guard("F")
        pl.op("tensor", lambda e: e.matmul(psE[:, 0:width], ones_row, rA[:, 0:width], start=True, stop=True), inc="sPE")
        pl.op("tensor", lambda e: e.matmul(psF[:, 0:width], ones_row, rC[:, 0:width], start=True, stop=True), inc="sPE")
        nd = []
        for k in range(NCH):
            pl.wait("vector", "sPE")
            pl.op("vector", lambda e, k=k: e.tensor_tensor(xsqb_t[k % 2][:, 0:width], xbf8_t[k][:, 0:width], psE[:, 0:width], ALU.subtract), inc="sDVE")
            pl.wait("vector", "sDVE")
            pl.op("vector", lambda e, k=k: e.tensor_tensor(xlnout[k][:, 0:width], xsqb_t[k % 2][:, 0:width], psF[:, 0:width], ALU.mult), inc="sDVE")
            nd.append(pl.val("sDVE"))
        norm_hist[0] = nd
        bank_done("E", "sDVE")
        bank_done("F", "sDVE")

    def hsrc(k):
        return x_in[k * 128:(k + 1) * 128, 0:TC]

    ln_block(hsrc, HALO, xln_t, loadw=TC)
    for k in range(NCH):
        r = slice(k * 128, (k + 1) * 128)
        pl.op("gpsimd", lambda e, d=win_sb[k], s=winT[r, :]: e.dma_start(out=d, in_=s), inc="sW", inc_n=16)
        pl.op("gpsimd", lambda e, d=wdt_sb[k], s=wdtT[r, :]: e.dma_start(out=d, in_=s), inc="sW", inc_n=16)
        pl.op("gpsimd", lambda e, d=rout_sb[k], s=routT[r, :]: e.dma_start(out=d, in_=s), inc="sW", inc_n=16)
    w_loaded = pl.val("sW")
    emit_precast(8)
    pl.wait("tensor", "sW", w_loaded)
    for m in range(NCH):
        pl.wait("tensor", "sDVE")
        pe_guard("A")
        for k in range(NCH):
            pl.op("tensor", lambda e, m=m, k=k: e.matmul(
                psA[:, 0:HALO], win_sb[k][:, m * 128:(m + 1) * 128], xln_t[k][:, 0:HALO],
                start=(k == 0), stop=(k == NCH - 1)), inc="sPE")
        pl.wait("vector", "sPE")
        pl.op("vector", lambda e, m=m: e.tensor_copy(tails[m], psA[:, 0:HALO]), inc="sDVE")
        bank_done("A", "sDVE")

    def z_piece(c, j):
        ccols = slice(c * TC, (c + 1) * TC)
        m0, m1 = 8 + 2 * j, 8 + 2 * j + 1
        pe_guard("E")
        pe_guard("F")
        for k in range(NCH):
            pl.op("tensor", lambda e, k=k, m0=m0: e.matmul(
                psE, win_sb[k][:, m0 * 128:(m0 + 1) * 128], xln_t[k],
                start=(k == 0), stop=(k == NCH - 1)), inc="sPE")
            pl.op("tensor", lambda e, k=k, m1=m1: e.matmul(
                psF, win_sb[k][:, m1 * 128:(m1 + 1) * 128], xln_t[k],
                start=(k == 0), stop=(k == NCH - 1)), inc="sPE")
        for half, psh in ((0, "E"), (1, "F")):
            k2 = 2 * j + half
            sl = k2 % 4
            pl.wait("scalar", "sPE")
            pl.wait("scalar", f"rZ{sl}")
            pl.op("scalar", lambda e, k2=k2, sl=sl, p=PS[psh]: e.activation(srb_t[sl], p, AF.Silu), inc="sACT")
            bank_done(psh, "sACT")
            pl.wait("sync", "sACT")
            pl.op("sync", lambda e, k2=k2, sl=sl, ccols=ccols: e.dma_start(
                out=g2_d[k2 * 128:(k2 + 1) * 128, ccols], in_=srb_t[sl]), inc=f"rZ{sl}", inc_n=16)
            pl.wait("vector", "sACT")
            pl.wait("vector", f"rZ{sl}")
            pl.op("vector", lambda e, k2=k2, sl=sl: e.tensor_scalar(g1s_t[sl], xcb_t[k2], dp_t[k2][:, 0:1], None, ALU.mult), inc="sDVE")
            pl.wait("vector", "sDVE")
            pl.op("vector", lambda e, k2=k2, sl=sl: e.tensor_mul(g1s_t[sl], g1s_t[sl], srb_t[sl]), inc="sDVE")
            pl.wait("sync", "sDVE")
            pl.op("sync", lambda e, k2=k2, sl=sl, ccols=ccols: e.dma_start(
                out=g1_d[k2 * 128:(k2 + 1) * 128, ccols], in_=g1s_t[sl]), inc=f"rZ{sl}", inc_n=16)

    def z_section(c):
        for j in range(4):
            z_piece(c, j)

    dlt_done = [0]
    for c in range(NCHUNK):
        x0 = HALO + c * TC

        def csrc(k, x0=x0):
            return x_in[k * 128:(k + 1) * 128, x0:x0 + TC]

        ln_block(csrc, TC, xln_t)
        emit_precast(10)

        xln_ready = pl.val("sDVE")
        for m in range(8):
            if m == 0:
                pl.wait("tensor", "sDVE", xln_ready)
            ps = "A" if m % 2 == 0 else "B"
            pe_guard(ps)
            for k in range(NCH):
                pl.op("tensor", lambda e, m=m, k=k, p=PS[ps]: e.matmul(
                    p, win_sb[k][:, m * 128:(m + 1) * 128], xln_t[k],
                    start=(k == 0), stop=(k == NCH - 1)), inc="sPE")
            if True:
                pl.wait("vector", "sPE")
                pl.wait("vector", "sACT")
                pl.op("vector", lambda e, m=m, p=PS[ps]: e.tensor_copy(xp515[m % 2][:, HALO:], p), inc="sDVE")
                bank_done(ps, "sDVE")
                pl.op("vector", lambda e, m=m: e.tensor_copy(xp515[m % 2][:, 0:HALO], tails[m]), inc="sDVE")
                pl.wait("vector", "sDVE")
                pl.op("vector", lambda e, m=m: e.tensor_scalar(
                    cacc_t[m % 2], xp515[m % 2][:, 0:TC], convw_t[m][:, 0:1], None, ALU.mult), inc="sDVE")
                for j in range(1, 4):
                    pl.wait("vector", "sDVE")
                    pl.op("vector", lambda e, m=m, j=j: e.scalar_tensor_tensor(
                        cacc_t[m % 2], xp515[m % 2][:, j:j + TC], convw_t[m][:, j:j + 1],
                        cacc_t[m % 2], ALU.mult, ALU.add), inc="sDVE")
                pl.op("vector", lambda e, m=m: e.tensor_copy(tails[m], xp515[m % 2][:, TC:TC + HALO]), inc="sDVE")
                pl.wait("scalar", "sDVE")
                pl.op("scalar", lambda e, m=m: e.activation(xc_t[m % 2], cacc_t[m % 2], AF.Silu, bias=convb_t[m][:, 0:1]), inc="sACT")
                pl.wait("vector", "sACT")
                pl.op("vector", lambda e, m=m: e.tensor_copy(xcb_t[m], xc_t[m % 2]), inc="sDVE")
        xcb_ready = pl.val("sDVE")
        for m in range(NCH):
            if m == 0:
                pl.wait("tensor", "sDVE", xcb_ready)
            ps = "A" if m % 2 == 0 else "B"
            pe_guard(ps)
            for k in range(NCH):
                pl.op("tensor", lambda e, m=m, k=k, p=PS[ps]: e.matmul(
                    p, wdt_sb[k][:, m * 128:(m + 1) * 128], xcb_t[k],
                    start=(k == 0), stop=(k == NCH - 1)), inc="sPE")
            pl.wait("scalar", "sPE")
            if dlt_done[0]:
                pl.wait("scalar", "sDVE", dlt_done[0])
            pl.op("scalar", lambda e, m=m, p=PS[ps]: e.activation(sp_t[0], p, AF.Relu, bias=bdt_t[m][:, 0:1]), inc="sACT")
            pl.op("scalar", lambda e, m=m, p=PS[ps]: e.activation(sp_t[1], p, AF.Abs, bias=bdt_t[m][:, 0:1]), inc="sACT")
            bank_done(ps, "sACT")
            pl.wait("scalar", "sACT")
            pl.op("scalar", lambda e: e.activation(sp_t[2], sp_t[1], AF.Exp, scale=-1.0), inc="sACT")
            pl.wait("scalar", "sACT")
            pl.op("scalar", lambda e: e.activation(sp_t[3], sp_t[2], AF.Ln, bias=1.0), inc="sACT")
            pl.wait("vector", "sACT")
            pl.wait("vector", "sW")
            pl.op("vector", lambda e, m=m: e.tensor_add(dlt_t[m % 2], sp_t[0], sp_t[3]), inc="sDVE")
            dlt_done[0] = pl.val("sDVE")
            pl.wait("gpsimd", "sDVE")
            pl.op("gpsimd", lambda e, m=m, c=c: e.dma_start(
                out=delta_d[m * 128:(m + 1) * 128, c * TC:(c + 1) * TC], in_=dlt_t[m % 2]), inc="sW", inc_n=16)
            pl.wait("vector", "sDVE")
            pl.wait("vector", "sS")
            pl.op("vector", lambda e, m=m, c=c: e.tensor_mul(
                wp_t[m % 2], dlt_t[m % 2], xcb_t[m]), inc="sDVE")
            pl.wait("sync", "sDVE")
            pl.op("sync", lambda e, m=m, c=c: e.dma_start(
                out=w_d[m * 128:(m + 1) * 128, c * TC:(c + 1) * TC], in_=wp_t[m % 2]), inc="sS", inc_n=16)
            pl.op("vector", lambda e, m=m: e.tensor_tensor_scan(
                cum_t[m % 2], dlt_t[m % 2], zero512, 0.0, ALU.add, ALU.add), inc="sDVE")
            pl.wait("vector", "sDVE")
            pl.op("vector", lambda e, m=m: e.tensor_scalar(
                sfx_t[m % 2], cum_t[m % 2], cum_t[m % 2][:, TC - 1:TC], None, ALU.subtract), inc="sDVE")
            pl.wait("scalar", "sDVE")
            pl.op("scalar", lambda e, m=m: e.activation(pc_t[m], negp_t[m], AF.Exp, scale=cum_t[m % 2][:, TC - 1:TC]), inc="sACT")
            for n in range(NST):
                b = n % 6
                pl.wait("scalar", "rBF", max(0, pl.val("rBF") - 5))
                pl.op("scalar", lambda e, m=m, n=n, b=b: e.activation(
                    bexp_t[b], sfx_t[m % 2], AF.Exp, scale=pexp_t[m][:, n:n + 1]), inc="rB")
                pl.wait("vector", "rB")
                pl.op("vector", lambda e, m=m, n=n, b=b: e.scalar_tensor_tensor(
                    bexp_t[b], bexp_t[b], 1.0, wp_t[m % 2], ALU.mult, ALU.mult,
                    accum_out=bccum[:, n:n + 1]), inc="rBF")
            pl.wait("vector", "sACT")
            pl.wait("vector", "rBF")
            pl.op("vector", lambda e, m=m: e.tensor_mul(btil[m], btil[m], pc_t[m]), inc="sDVE")
            pl.wait("vector", "sDVE")
            pl.op("vector", lambda e, m=m: e.tensor_add(btil[m], btil[m], bccum), inc="sDVE")
            pl.op("vector", lambda e, m=m: e.tensor_tensor(dtot[m], dtot[m], cum_t[m % 2][:, TC - 1:TC], ALU.add), inc="sDVE")
        if c < NCHUNK - 1:
            z_section(c)

    emit_precast(99)
    wcast_done = pl.val("sWC")
    pl.wait("vector", "sDVE")
    pl.op("vector", lambda e: e.memset(dtotall, 0.0), inc="sDVE")
    pl.wait("vector", "sDVE")
    for m in range(NCH):
        pl.op("vector", lambda e, m=m: e.tensor_copy(dtotall[:, m:m + 1], dtot[m]),
              inc="sDVE" if m == NCH - 1 else None)
    pl.wait("sync", "sDVE")
    pl.op("sync", lambda e: e.dma_start(out=cin_d[DI:DI + 128, 0:NST], in_=dtotall), inc="sS", inc_n=16)
    for m in range(NCH):
        r = slice(m * 128, (m + 1) * 128)
        pl.op("sync", lambda e, m=m, r=r: e.dma_start(out=cin_d[r, 0:NST], in_=btil[m]), inc="sS", inc_n=16)
        if debug:
            pl.op("sync", lambda e, m=m, r=r: e.dma_start(out=dbg_b[r, 0:NST], in_=btil[m]), inc="sS", inc_n=16)
    pl.wait("gpsimd", "sS")
    pl.op("gpsimd", lambda e: e.collective_compute(
        "AllGather", ALU.bypass, replica_groups=[list(range(NCORE))],
        ins=[cin_d], outs=[cout_d]), inc="sCC")

    z_section(NCHUNK - 1)
    g12_done = [(f"rZ{s}", pl.val(f"rZ{s}")) for s in range(4)]

    pl.wait("vector", "sDVE")
    for m in range(NCH):
        pl.op("vector", lambda e, m=m: e.memset(hsall[m][:, 0:NST], 0.0), inc="sDVE")
    pl.wait("sync", "sCC")
    chain_av, chain_dv = {}, {}
    for j in range(NCORE - 1):
        b2i = j % 2
        base = j * (DI + 128)
        srcB = cout_d[base:base + DI, :].rearrange("(c p) f -> p c f", p=128)
        bdj3 = bdj2[b2i].rearrange("p (c f) -> p c f", c=NCH)
        if j >= 2:
            pl.wait("sync", "sACT", chain_av[j - 2])
            pl.wait("sync", "sDVE", chain_dv[j - 2])
        pl.op("sync", lambda e, s=srcB, d=bdj3: e.dma_start(out=d, in_=s), inc=f"sXT{b2i}", inc_n=16)
        pl.op("sync", lambda e, s=cout_d[base + DI:base + DI + 128, 0:NCH], d=bdjD2[b2i]: e.dma_start(out=d, in_=s), inc=f"sXT{b2i}", inc_n=16)
        pl.wait("scalar", f"sXT{b2i}")
        pl.wait("scalar", "sDVE")
        for m in range(NCH):
            pl.op("scalar", lambda e, m=m, b2i=b2i: e.activation(
                pc_t[m], negp_t[m], AF.Exp,
                scale=bdjD2[b2i][:, m:m + 1]), inc="sACT")
        chain_av[j] = pl.val("sACT")
        pl.wait("vector", "sACT")
        for m in range(NCH):
            pl.op("vector", lambda e, m=m, j=j: e.tensor_mul(
                hsall[m][:, (j + 1) * NST:(j + 2) * NST], hsall[m][:, j * NST:(j + 1) * NST], pc_t[m]), inc="sDVE")
            pl.wait("vector", "sDVE")
            pl.op("vector", lambda e, m=m, j=j, b2i=b2i: e.tensor_tensor(
                hsall[m][:, (j + 1) * NST:(j + 2) * NST], hsall[m][:, (j + 1) * NST:(j + 2) * NST],
                bdj2[b2i][:, m * NST: (m + 1) * NST], ALU.add), inc="sDVE")
        chain_dv[j] = pl.val("sDVE")

    sdve_h = pl.sem("sDVE")

    def emit_switch(v):
        for arm in v.switch_core_id(n=128):
            lg = arm.logical
            sel = (lg % NCORE) if lg is not None else 0
            v.memset(tiny, 1000.0 + sel)
            for m in range(NCH):
                ins = v.tensor_copy(state[m], hsall[m][:, sel * NST:(sel + 1) * NST])
                if m == NCH - 1:
                    ins.then_inc(sdve_h, 1)

    pl.wait("vector", "sDVE")
    if debug:
        pl.wait("sync", "sDVE")
        pl.op("sync", lambda e: e.dma_start(out=dbg_hsall, in_=hsall[0]), inc="sS", inc_n=16)
    pl.op("vector", emit_switch)
    pl.cnt["sDVE"] += 1
    svsel = pl.val("sDVE")
    if debug:
        pl.wait("sync", "sDVE", svsel)
        for m in range(NCH):
            r = slice(m * 128, (m + 1) * 128)
            pl.op("sync", lambda e, m=m, r=r: e.dma_start(out=dbg_hs[r, :], in_=state[m]), inc="sS", inc_n=16)
        pl.op("sync", lambda e: e.dma_start(out=dbg_bdj[0:128, 0:1], in_=tiny, allow_slow_non_contiguous=True), inc="sS", inc_n=16)

    p1_pe = pl.val("sPE")
    pl.wait("gpsimd", "sPE", p1_pe)
    for k in range(NCH):
        r = slice(k * 128, (k + 1) * 128)
        pl.op("gpsimd", lambda e, d=wout_sb[k], s=woutT[r, :]: e.dma_start(out=d, in_=s), inc="sW", inc_n=16)
    wout_loaded = pl.val("sW")
    pl.wait("sync", "sW")
    pl.wait("sync", "sS")
    pl.wait("sync", "sWC", wcast_done)
    pl.wait("gpsimd", "sWC", wcast_done)

    gate_all = {}
    wout_pe = {}
    xn2_ok = {}
    lnrt_pe = {}
    gate_vals = {}
    x1_vals = {}
    ffn_h_done = {}
    ffn_red_done = {}
    ffn_z_act = {}
    ffn_w2_ready = {}
    me_store_val = {}
    last_tree = [pl.val("sDVE")]

    def scan_block(t, m):
        tc0 = t * TC
        d = m % 2
        pl.wait("sync", "rA", max(0, pl.val("rA") - 4))
        pl.op("sync", lambda e, m=m, d=d, tc0=tc0: e.dma_start(
            out=dltb_t[d], in_=delta_d[m * 128:(m + 1) * 128, tc0:tc0 + TC]), inc=f"rDL{d}", inc_n=16)
        pl.wait("sync", "rAF", max(0, pl.val("rAF") - 4))
        pl.op("sync", lambda e, m=m, d=d, tc0=tc0: e.dma_start(
            out=wr_t[d], in_=w_d[m * 128:(m + 1) * 128, tc0:tc0 + TC]), inc=f"rWR{d}", inc_n=16)
        pl.wait("vector", "sDVE", last_tree[0])
        if t == 0 and m == 0:
            pl.wait("vector", "sDVE", svsel)
        pl.wait("vector", f"rWR{d}")
        for g in range(4):
            pl.wait("scalar", "rAF", max(0, pl.val("rAF") - 1))
            if g == 0:
                pl.wait("scalar", f"rDL{d}")
            for i in range(4):
                n = g * 4 + i
                a = n % 8
                pl.op("scalar", lambda e, m=m, n=n, a=a, d=d: e.activation(
                    da_t[a], dltb_t[d], AF.Exp, scale=negp_t[m][:, n:n + 1]),
                    inc="rA" if i == 3 else None)
            pl.wait("vector", "rA")
            for i in range(4):
                n = g * 4 + i
                a = n % 8
                pl.op("vector", lambda e, m=m, n=n, a=a, d=d: e.tensor_tensor_scan(
                    gbuf[:, n * TC:(n + 1) * TC], da_t[a], wr_t[d],
                    state[m][:, n:n + 1], ALU.mult, ALU.add),
                    inc="rAF" if i == 3 else None)
        pl.wait("vector", "rAF")
        gtail = gbuf.rearrange("p (n t) -> p n t", n=NST)[:, :, TC - 1:TC]
        pl.op("vector", lambda e, m=m, g=gtail: e.tensor_copy(
            state[m].unsqueeze(2), g), inc="sDVE")
        pl.wait("vector", "sDVE")
        for n in range(NST):
            pl.op("vector", lambda e, m=m, n=n: e.tensor_scalar(
                gbuf[:, n * TC:(n + 1) * TC], gbuf[:, n * TC:(n + 1) * TC],
                bc_t[m][:, n:n + 1], None, ALU.mult),
                inc="sDVE" if n == NST - 1 else None)
        pl.wait("vector", "sDVE")
        for n in range(8):
            pl.op("vector", lambda e, n=n: e.tensor_add(
                gbuf[:, (2 * n) * TC:(2 * n + 1) * TC],
                gbuf[:, (2 * n) * TC:(2 * n + 1) * TC],
                gbuf[:, (2 * n + 1) * TC:(2 * n + 2) * TC]),
                inc="sDVE" if n == 7 else None)
        pl.wait("vector", "sDVE")
        for n in range(4):
            pl.op("vector", lambda e, n=n: e.tensor_add(
                gbuf[:, (4 * n) * TC:(4 * n + 1) * TC],
                gbuf[:, (4 * n) * TC:(4 * n + 1) * TC],
                gbuf[:, (4 * n + 2) * TC:(4 * n + 3) * TC]),
                inc="sDVE" if n == 3 else None)
        pl.wait("vector", "sDVE")
        for n in range(2):
            pl.op("vector", lambda e, n=n: e.tensor_add(
                gbuf[:, (8 * n) * TC:(8 * n + 1) * TC],
                gbuf[:, (8 * n) * TC:(8 * n + 1) * TC],
                gbuf[:, (8 * n + 4) * TC:(8 * n + 5) * TC]),
                inc="sDVE" if n == 1 else None)
        pl.wait("vector", "sDVE")
        pl.op("vector", lambda e: e.tensor_tensor(
            y_t, gbuf[:, 0:TC], gbuf[:, 8 * TC:9 * TC], ALU.add), inc="sDVE")
        g4 = (t * NCH + m) % 3
        gv = gate_vals.get((t, m - 3), 0) if m >= 3 else gate_vals.get((t - 1, m + 5), 0)
        if gv:
            pl.wait("sync", "sDVE", gv)
        if t == 0 and m < 3:
            for gs, gv2 in g12_done:
                pl.wait("sync", gs, gv2)
        pl.op("sync", lambda e, m=m, g4=g4, tc0=tc0: e.dma_start(
            out=g2r_t[g4], in_=g2_d[m * 128:(m + 1) * 128, tc0:tc0 + TC]), inc=f"rGG{g4}", inc_n=16)
        pl.op("sync", lambda e, m=m, g4=g4, tc0=tc0: e.dma_start(
            out=g1r_t[g4], in_=g1_d[m * 128:(m + 1) * 128, tc0:tc0 + TC]), inc=f"rGG{g4}", inc_n=16)
        pl.wait("vector", f"rGG{g4}")
        if t >= 2:
            pl.wait("vector", "sPE", wout_pe.get(t - 2, 0))
        pl.wait("vector", "sDVE")
        pl.op("vector", lambda e, m=m, g4=g4: e.tensor_mul(y_t, y_t, g2r_t[g4]), inc="sDVE")
        pl.wait("vector", "sDVE")
        pl.op("vector", lambda e, m=m, g4=g4, t=t: e.tensor_tensor(
            gated_t[t % 2][m], y_t, g1r_t[g4], ALU.add), inc="sDVE")
        gate_vals[(t, m)] = pl.val("sDVE")
        last_tree[0] = pl.val("sDVE")
        if debug:
            pl.wait("sync", "sDVE")
            pl.op("sync", lambda e, m=m, t=t, tc0=tc0: e.dma_start(
                out=dbg_y[m * 128:(m + 1) * 128, tc0:tc0 + TC], in_=gated_t[t % 2][m]), inc="sS", inc_n=16)

    def wout_piece(t, mt):
        tc0 = t * TC
        ps = "A" if mt % 2 == 0 else "B"
        if mt == 0:
            pl.wait("tensor", "sW", wout_loaded)
            pl.wait("tensor", "sDVE", gate_all[t])
        pe_guard(ps)
        for k in range(NCH):
            pl.op("tensor", lambda e, mt=mt, k=k, p=PS[ps], t=t: e.matmul(
                p, wout_sb[k][:, mt * 128:(mt + 1) * 128], gated_t[t % 2][k],
                start=(k == 0), stop=(k == NCH - 1)), inc="sPE")
        xv = x1_vals.get((t, mt - 2), 0) if mt >= 2 else x1_vals.get((t - 1, mt + 6), 0)
        if xv:
            pl.wait("sync", "sDVE", xv)
        pl.op("sync", lambda e, mt=mt, tc0=tc0: e.dma_start(
            out=xr_t[mt % 2], in_=x_in[mt * 128:(mt + 1) * 128, HALO + tc0:HALO + tc0 + TC]), inc=f"rXR{mt % 2}", inc_n=16)
        pl.wait("vector", f"rXR{mt % 2}")
        pl.wait("vector", "sPE")
        pl.op("vector", lambda e, mt=mt, p=PS[ps], t=t: e.tensor_tensor(
            x1_t[t % 2][mt], p, xr_t[mt % 2], ALU.add), inc="sDVE")
        x1_vals[(t, mt)] = pl.val("sDVE")
        bank_done(ps, "sDVE")
        if debug:
            pl.wait("sync", "sDVE")
            pl.op("sync", lambda e, mt=mt, tc0=tc0, t=t: e.dma_start(
                out=dbg_x1[mt * 128:(mt + 1) * 128, tc0:tc0 + TC], in_=x1_t[t % 2][mt]), inc="sS", inc_n=16)

    def ln2_router(t):
        tc0 = t * TC
        tb2 = t % 2
        for k in range(NCH):
            pl.wait("scalar", "sDVE")
            pl.wait("scalar", "sPE")
            pl.op("scalar", lambda e, k=k, tb2=tb2: e.activation(x1sq_t[k % 2], x1_t[tb2][k], AF.Square), inc="sACT")
            pl.wait("tensor", "sACT")
            if k < 2:
                pe_guard("A")
                pe_guard("B")
            pl.op("tensor", lambda e, k=k, tb2=tb2: e.matmul(psA[0:1, :], ones_k, x1_t[tb2][k],
                                                    start=(k == 0), stop=(k == NCH - 1)), inc="sPE")
            pl.op("tensor", lambda e, k=k: e.matmul(psB[32:33, :], ones_k, x1sq_t[k % 2],
                                                    start=(k == 0), stop=(k == NCH - 1)), inc="sPE")
        pl.wait("vector", "sPE")
        pl.op("vector", lambda e: e.tensor_scalar(rA, psA[0:1, :], 1.0 / DM, None, ALU.mult), inc="sDVE")
        pl.op("vector", lambda e: e.tensor_scalar(rB, psB[32:33, :], 1.0 / DM, None, ALU.mult), inc="sDVE")
        bank_done("A", "sDVE")
        bank_done("B", "sDVE")
        pl.wait("vector", "sDVE")
        pl.op("vector", lambda e: e.tensor_mul(rC, rA, rA), inc="sDVE")
        pl.wait("vector", "sDVE")
        pl.op("vector", lambda e: e.tensor_tensor(rB, rB, rC, ALU.subtract), inc="sDVE")
        pl.wait("scalar", "sDVE")
        pl.op("scalar", lambda e: e.activation(rB, rB, AF.Sqrt, bias=EPS), inc="sACT")
        pl.wait("vector", "sACT")
        pl.op("vector", lambda e: e.reciprocal(rC, rB), inc="sDVE")
        pl.wait("tensor", "sDVE")
        pe_guard("A")
        pe_guard("B")
        pl.op("tensor", lambda e: e.matmul(psA, ones_row, rA, start=True, stop=True), inc="sPE")
        pl.op("tensor", lambda e: e.matmul(psB, ones_row, rC, start=True, stop=True), inc="sPE")
        for k in range(NCH):
            pl.wait("vector", "sPE")
            pl.wait("vector", "sS")
            pl.op("vector", lambda e, k=k, tb2=tb2: e.tensor_tensor(yout_t[k % 2], x1_t[tb2][k], psA, ALU.subtract), inc="sDVE")
            pl.wait("vector", "sDVE")
            pl.op("vector", lambda e, k=k, tb2=tb2: e.tensor_tensor(xn2_t[tb2][k], yout_t[k % 2], psB, ALU.mult), inc="sDVE")
        bank_done("A", "sDVE")
        bank_done("B", "sDVE")
        xn2_ok[t] = pl.val("sDVE")
        pl.wait("tensor", "sDVE", xn2_ok[t])
        pe_guard("A")
        for tb in range(4):
            for k in range(NCH):
                pl.op("tensor", lambda e, tb=tb, k=k, tb2=tb2: e.matmul(
                    psA[:, tb * NEXP:(tb + 1) * NEXP], xn2_t[tb2][k][:, tb * 128:(tb + 1) * 128],
                    rout_sb[k], start=(k == 0), stop=(k == NCH - 1)), inc="sPE")
        pl.wait("vector", "sPE")
        for tb in range(4):
            pl.op("vector", lambda e, tb=tb: e.tensor_tensor(
                lg_t[tb], psA[:, tb * NEXP:(tb + 1) * NEXP], rb_t, ALU.add), inc="sDVE")
        bank_done("A", "sDVE")
        for tb in range(4):
            pl.wait("vector", "sDVE")
            pl.op("vector", lambda e, tb=tb: e.tensor_copy(lgw_t[tb], lg_t[tb]), inc="sDVE")
            for it in range(TOPK - 1):
                pl.wait("vector", "sDVE")
                pl.op("vector", lambda e, tb=tb: e.reduce_max(mx_t, lgw_t[tb], axis=mybir.AxisListType.X), inc="sDVE")
                pl.wait("vector", "sDVE")
                pl.op("vector", lambda e, tb=tb: e.tensor_scalar(
                    msk_t[tb], lgw_t[tb], mx_t[:, 0:1], None, ALU.is_ge), inc="sDVE")
                pl.wait("vector", "sDVE")
                pl.op("vector", lambda e, tb=tb: e.scalar_tensor_tensor(
                    lgw_t[tb], msk_t[tb], -BIG, lgw_t[tb], ALU.mult, ALU.add), inc="sDVE")
            pl.wait("vector", "sDVE")
            pl.op("vector", lambda e, tb=tb: e.reduce_max(m4_t[tb], lgw_t[tb], axis=mybir.AxisListType.X), inc="sDVE")
            pl.wait("vector", "sDVE")
            pl.op("vector", lambda e, tb=tb: e.tensor_scalar(
                msk_t[tb], lg_t[tb], m4_t[tb][:, 0:1], None, ALU.is_ge), inc="sDVE")
        if debug:
            pl.wait("sync", "sDVE")
            for tb in range(4):
                pl.op("sync", lambda e, tb=tb, tc0=tc0: e.dma_start(
                    out=dbg_msk[tc0 + tb * 128: tc0 + (tb + 1) * 128, :], in_=msk_t[tb]), inc="sS", inc_n=16)
            for k in range(NCH):
                pl.op("sync", lambda e, k=k, tc0=tc0, tb2=tb2: e.dma_start(
                    out=dbg_xn2[k * 128:(k + 1) * 128, tc0:tc0 + TC], in_=xn2_t[tb2][k]), inc="sS", inc_n=16)
            for tb in range(4):
                pl.op("sync", lambda e, tb=tb, tc0=tc0: e.dma_start(
                    out=dbg_lg[tc0 + tb * 128: tc0 + (tb + 1) * 128, :], in_=lg_t[tb]), inc="sS", inc_n=16)
        pl.wait("tensor", "sDVE")
        pe_guard("B")
        for tb in range(4):
            pl.op("tensor", lambda e, tb=tb: e.transpose(
                psB[0:NEXP, tb * 128:(tb + 1) * 128], msk_t[tb], ident), inc="sPE")
        pl.wait("vector", "sPE")
        pl.op("vector", lambda e: e.tensor_scalar(me_t, psB[0:NEXP, :], BIG, -BIG, ALU.mult, ALU.add), inc="sDVE")
        bank_done("B", "sDVE")
        pl.wait("sync", "sDVE")
        pl.op("sync", lambda e, tb2=tb2: e.dma_start(out=me_d[tb2], in_=me_t), inc="sMD", inc_n=16)
        me_store_val[t] = pl.val("sMD")
        lnrt_pe[t] = pl.val("sPE")

    def ffn_step(t, s):
        tc0 = t * TC
        tb2 = t % 2
        q = s // 2
        if s % 2 == 0:
            wv = ffn_h_done.get((t, q - 2), 0) if q >= 2 else ffn_h_done.get((t - 1, q + 2), 0)
            if wv:
                pl.wait("gpsimd", "sPE", wv)
            for k in range(NCH):
                pl.op("gpsimd", lambda e, q=q, k=k: e.dma_start(
                    out=w1q_sb[q % 2][k], in_=w1b_d[k * 128:(k + 1) * 128, q * (HID // 4):(q + 1) * (HID // 4)]), inc=f"rW1{q % 2}", inc_n=16)
            rv = ffn_red_done.get((t, q - 1), 0) if q >= 1 else ffn_red_done.get((t - 1, 3), 0)
            if rv:
                pl.wait("gpsimd", "sPE", rv)
            for hs in range(8):
                pl.op("gpsimd", lambda e, q=q, hs=hs: e.dma_start(
                    out=w2h_sb[hs], in_=w2b_d[q * (HID // 4) + hs * 128: q * (HID // 4) + (hs + 1) * 128, :]), inc="rW2", inc_n=16)
            pl.wait("gpsimd", "sMD", me_store_val[t])
            for i in range(4):
                pl.op("gpsimd", lambda e, q=q, i=i, tb2=tb2: e.dma_start(
                    out=me_row4[0:1, i * TC:(i + 1) * TC], in_=me_d[tb2][q * 4 + i:q * 4 + i + 1, :]), inc="sME", inc_n=16)
            w1_ready = pl.val(f"rW1{q % 2}")
            ffn_w2_ready[(t, q)] = pl.val("rW2")
            pen_ready = pl.val("sME")
            if rv:
                pl.wait("scalar", "sPE", rv)
            for hs in range(8):
                hb = "C" if hs % 2 == 0 else "D"
                pl.wait("tensor", f"rW1{q % 2}", w1_ready)
                pl.wait("tensor", "sME", pen_ready)
                pl.wait("tensor", "sDVE", xn2_ok[t])
                pe_guard(hb)
                for k in range(NCH):
                    pl.op("tensor", lambda e, k=k, hs=hs, q=q, p=PS[hb], tb2=tb2: e.matmul(
                        p, w1q_sb[q % 2][k][:, hs * 128:(hs + 1) * 128], xn2_t[tb2][k],
                        start=(k == 0), stop=False), inc="sPE")
                pl.op("tensor", lambda e, hs=hs, p=PS[hb]: e.matmul(
                    p, ones_rowb, me_row4[0:1, (hs // 2) * TC:(hs // 2 + 1) * TC],
                    start=False, stop=True), inc="sPE")
                pl.wait("scalar", "sPE")
                pl.op("scalar", lambda e, q=q, hs=hs, p=PS[hb]: e.activation(
                    z_t[hs], p, AF.Relu, bias=b1_t[:, q * 8 + hs: q * 8 + hs + 1]), inc="sACT")
                bank_done(hb, "sACT")
            ffn_h_done[(t, q)] = pl.val("sPE")
            ffn_z_act[(t, q)] = pl.val("sACT")
        else:
            for ot in range(NCH):
                yb = "E" if ot % 2 == 0 else "F"
                pe_guard(yb)
                if ot == 0:
                    pl.wait("tensor", "sACT", ffn_z_act[(t, q)])
                    pl.wait("tensor", "rW2", ffn_w2_ready[(t, q)])
                for hs in range(8):
                    pl.op("tensor", lambda e, ot=ot, hs=hs, p=PS[yb]: e.matmul(
                        p, w2h_sb[hs][:, ot * 128:(ot + 1) * 128], z_t[hs],
                        start=(hs == 0), stop=(hs == 7)), inc="sPE")
                pl.wait("vector", "sPE")
                if q == 0:
                    pl.op("vector", lambda e, ot=ot, p=PS[yb], tb2=tb2: e.scalar_tensor_tensor(
                        ysb_t[ot], p, b2_t[:, ot:ot + 1], x1_t[tb2][ot], ALU.add, ALU.add), inc="sDVE")
                elif q < 3:
                    pl.op("vector", lambda e, ot=ot, p=PS[yb]: e.tensor_tensor(
                        ysb_t[ot], p, ysb_t[ot], ALU.add), inc="sDVE")
                else:
                    pl.wait("vector", "sS")
                    pl.op("vector", lambda e, ot=ot, p=PS[yb]: e.tensor_tensor(
                        yout_t[ot % 2], p, ysb_t[ot], ALU.add), inc="sDVE")
                    pl.wait("sync", "sDVE")
                    pl.op("sync", lambda e, ot=ot, tc0=tc0: e.dma_start(
                        out=out_d[ot * 128:(ot + 1) * 128, tc0:tc0 + TC], in_=yout_t[ot % 2]), inc="sS", inc_n=16)
                bank_done(yb, "sDVE")
            ffn_red_done[(t, q)] = pl.val("sPE")

    for t in range(NCHUNK + 1):
        for m in range(NCH):
            if t < NCHUNK:
                scan_block(t, m)
            if m == 0:
                if t >= 1:
                    for mt in range(NCH):
                        wout_piece(t - 1, mt)
                    wout_pe[t - 1] = pl.val("sPE")
                if t >= 2:
                    ffn_step(t - 2, 6)
            elif m == 1:
                if t >= 1:
                    ln2_router(t - 1)
                if t >= 2:
                    ffn_step(t - 2, 7)
            else:
                if t >= 1:
                    ffn_step(t - 1, m - 2)
        if t < NCHUNK:
            gate_all[t] = gate_vals[(t, NCH - 1)]
    ffn_step(NCHUNK - 1, 6)
    ffn_step(NCHUNK - 1, 7)

    pl.wait("sync", "sS")
    pl.wait("sync", "sW")
    pl.wait("sync", "sX")

    with nc.Block() as block:
        @block.sync
        def _(e):
            for f in pl.ops["sync"]:
                f(e)

        @block.vector
        def _(e):
            for f in pl.ops["vector"]:
                f(e)

        @block.scalar
        def _(e):
            for f in pl.ops["scalar"]:
                f(e)

        @block.tensor
        def _(e):
            for f in pl.ops["tensor"]:
                f(e)

        @block.gpsimd
        def _(e):
            for f in pl.ops["gpsimd"]:
                f(e)

    stack.close()
    return nc


def kernel(**inputs):
    x = np.asarray(inputs["L0_attn_x"])
    xT = np.ascontiguousarray(x[0].T)

    shared = dict(
        winT=np.ascontiguousarray(np.asarray(inputs["L0_attn_W_in"]).T),
        wdtT=np.ascontiguousarray(np.asarray(inputs["L0_attn_W_dt"]).T),
        woutT=np.ascontiguousarray(np.asarray(inputs["L0_attn_W_out"]).T),
        w1T=np.ascontiguousarray(np.asarray(inputs["L0_ffn_w1"]).T),
        w2T=np.ascontiguousarray(np.asarray(inputs["L0_ffn_w2"]).T),
        routT=np.ascontiguousarray(np.asarray(inputs["L0_ffn_router_w"]).T),
        rout_b=np.ascontiguousarray(np.asarray(inputs["L0_ffn_router_b"]).reshape(1, NEXP)),
        conv_w=np.ascontiguousarray(np.asarray(inputs["L0_attn_conv_w"])),
        conv_b=np.ascontiguousarray(np.asarray(inputs["L0_attn_conv_b"]).reshape(DI, 1)),
        b_dt=np.ascontiguousarray(np.asarray(inputs["L0_attn_b_dt"]).reshape(DI, 1)),
        a_log=np.ascontiguousarray(np.asarray(inputs["L0_attn_A_log"])),
        bp=np.ascontiguousarray(np.asarray(inputs["L0_attn_Bp"])),
        cp=np.ascontiguousarray(np.asarray(inputs["L0_attn_Cp"])),
        dp=np.ascontiguousarray(np.asarray(inputs["L0_attn_Dp"]).reshape(DI, 1)),
        b1c=np.ascontiguousarray(np.asarray(inputs["L0_ffn_b1"]).reshape(HID // 128, 128).T),
        b2c=np.ascontiguousarray(np.asarray(inputs["L0_ffn_b2"]).reshape(DM // 128, 128).T),
    )
    in_maps = []
    for k in range(NCORE):
        lo = k * TOK
        if lo >= HALO:
            xs = np.ascontiguousarray(xT[:, lo - HALO: lo + TOK])
        else:
            xs = np.ascontiguousarray(
                np.concatenate([np.zeros((DM, HALO), np.float32), xT[:, :TOK]], axis=1))
        im = {"x": xs}
        im.update(shared)
        in_maps.append(im)

    nc = build_core()
    res = run_bass_kernel_spmd(nc, in_maps, core_ids=list(range(NCORE)))
    outs = [np.asarray(res.results[k]["out"]).T for k in range(NCORE)]
    return np.concatenate(outs, axis=0).reshape(1, SEQ, DM).astype(np.float32)


# version 23 (final)
# speedup vs baseline: 1.1843x; 1.1843x over previous
import numpy as np
import concourse.bass as bass
import concourse.mybir as mybir
from concourse.bass_utils import run_bass_kernel_spmd

F32 = mybir.dt.float32
BF16 = mybir.dt.bfloat16
I32 = mybir.dt.int32
U8 = mybir.dt.uint8
AF = mybir.ActivationFunctionType
ALU = mybir.AluOpType

NCORE = 8
DM = 1024
DI = 1024
NST = 16
SEQ = 16384
TOK = SEQ // NCORE
HALO = 3
NCH = DM // 128
TC = 512
NCHUNK = TOK // TC
HID = 4096
NEXP = 16
TOPK = 4
EPS = 1e-5
BIG = 1e30

ENGINES = ("sync", "vector", "scalar", "tensor", "gpsimd")


class Plan:
    def __init__(self, nc, stack):
        self.nc = nc
        self.stack = stack
        self.ops = {e: [] for e in ENGINES}
        self.cnt = {}
        self.sems = {}

    def sem(self, name):
        if name not in self.sems:
            self.sems[name] = self.stack.enter_context(self.nc.semaphore(name))
            self.cnt[name] = 0
        return self.sems[name]

    def op(self, eng, fn, inc=None, inc_n=1):
        if inc is not None:
            s = self.sem(inc)
            self.cnt[inc] += inc_n
            n = inc_n

            def wrapped(e, fn=fn, s=s, n=n):
                fn(e).then_inc(s, n)

            self.ops[eng].append(wrapped)
        else:
            self.ops[eng].append(lambda e, fn=fn: fn(e))

    def wait(self, eng, sem, val=None):
        self.sem(sem)
        v = self.cnt[sem] if val is None else val
        if v <= 0:
            return
        s = self.sems[sem]
        self.ops[eng].append(lambda e, s=s, v=v: e.wait_ge(s, v))

    def val(self, sem):
        self.sem(sem)
        return self.cnt[sem]


class Carver:
    def __init__(self, nc, total_bytes):
        self.t = nc.alloc_sbuf_tensor("arena", [128, total_bytes], U8)
        self.off = 0
        self.marks = []

    def view(self, shape, dtype):
        es = 4 if dtype in (F32, I32) else 2
        n = shape[1] * es
        o = self.off
        self.off = (self.off + n + 63) & ~63
        return self.t[0:shape[0], o:o + n].bitcast(dtype)

    def push(self):
        self.marks.append(self.off)

    def pop(self):
        self.off = self.marks.pop()


def build_core(debug=0):
    import contextlib
    nc = bass.Bass("TRN2", target_bir_lowering=False)
    stack = contextlib.ExitStack()

    def din(name, shape, dt=F32):
        return nc.dram_tensor(name, shape, dt, kind="ExternalInput").ap()

    x_in = din("x", [DM, HALO + TOK])
    winT = din("winT", [DM, 2 * DI])
    wdtT = din("wdtT", [DI, DI])
    woutT = din("woutT", [DI, DM])
    w1T = din("w1T", [DM, HID])
    w2T = din("w2T", [HID, DM])
    routT = din("routT", [DM, NEXP])
    rout_b = din("rout_b", [1, NEXP])
    conv_w = din("conv_w", [DI, 4])
    conv_b = din("conv_b", [DI, 1])
    b_dt = din("b_dt", [DI, 1])
    a_log = din("a_log", [DI, NST])
    bp_in = din("bp", [DI, NST])
    cp_in = din("cp", [DI, NST])
    dp_in = din("dp", [DI, 1])
    b1c = din("b1c", [128, HID // 128])
    b2c = din("b2c", [128, DM // 128])

    out_d = nc.dram_tensor("out", [DM, TOK], F32, kind="ExternalOutput").ap()
    okind = "ExternalOutput" if debug else "Internal"
    delta_d = nc.dram_tensor("delta_d", [DI, TOK], BF16, kind=okind).ap()
    g1_d = nc.dram_tensor("g1_d", [DI, TOK], BF16, kind=okind).ap()
    g2_d = nc.dram_tensor("g2_d", [DI, TOK], BF16, kind=okind).ap()
    w1b_d = nc.dram_tensor("w1b_d", [DM, HID], BF16).ap()
    w2b_d = nc.dram_tensor("w2b_d", [HID, DM], BF16).ap()
    cin_d = nc.dram_tensor("cin_d", [DI + 128, NST], F32).ap()
    me_d = [nc.dram_tensor(f"me_d{i}", [NEXP, TC], BF16).ap() for i in range(2)]
    w_d = nc.dram_tensor("w_d", [DI, TOK], BF16).ap()
    cout_d = nc.dram_tensor("cout_d", [NCORE * (DI + 128), NST], F32).ap()
    if debug:
        dbg_y = nc.dram_tensor("dbg_y", [DI, TOK], BF16, kind="ExternalOutput").ap()
        dbg_x1 = nc.dram_tensor("dbg_x1", [DM, TOK], BF16, kind="ExternalOutput").ap()
        dbg_b = nc.dram_tensor("dbg_b", [DI, NST + 1], F32, kind="ExternalOutput").ap()
        dbg_hs = nc.dram_tensor("dbg_hs", [DI, NST], F32, kind="ExternalOutput").ap()
        dbg_hsall = nc.dram_tensor("dbg_hsall", [128, NCORE * NST], F32, kind="ExternalOutput").ap()
        dbg_bdj = nc.dram_tensor("dbg_bdj", [128, NCH * (NST + 1)], F32, kind="ExternalOutput").ap()
        dbg_msk = nc.dram_tensor("dbg_msk", [TOK, NEXP], F32, kind="ExternalOutput").ap()
        dbg_lg = nc.dram_tensor("dbg_lg", [TOK, NEXP], F32, kind="ExternalOutput").ap()
        dbg_xn2 = nc.dram_tensor("dbg_xn2", [DM, TOK], BF16, kind="ExternalOutput").ap()

    cv = Carver(nc, 207 * 1024)
    convw_t = [cv.view([128, 4], F32) for _ in range(NCH)]
    convb_t = [cv.view([128, 1], F32) for _ in range(NCH)]
    bdt_t = [cv.view([128, 1], F32) for _ in range(NCH)]
    alog_t = [cv.view([128, NST], F32) for _ in range(NCH)]
    pexp_t = [cv.view([128, NST], F32) for _ in range(NCH)]
    negp_t = [cv.view([128, NST], F32) for _ in range(NCH)]
    bc_t = [cv.view([128, NST], F32) for _ in range(NCH)]
    dp_t = [cv.view([128, 1], F32) for _ in range(NCH)]
    b1_t = cv.view([128, HID // 128], F32)
    b2_t = cv.view([128, DM // 128], F32)
    rb_t = cv.view([128, NEXP], F32)
    rout_sb = [cv.view([128, NEXP], BF16) for _ in range(NCH)]
    ones_k = cv.view([128, 1], BF16)
    ones_kf = cv.view([128, 1], F32)
    ones_row = cv.view([1, 128], F32)
    ones_rowb = cv.view([1, 128], BF16)
    ident = cv.view([128, 128], F32)
    iotaA = cv.view([128, 128], I32)
    iotaB = cv.view([128, 128], I32)
    zero512 = cv.view([128, TC], F32)
    btil = [cv.view([128, NST], F32) for _ in range(NCH)]
    bccum = cv.view([128, NST], F32)
    pc_t = [cv.view([128, NST], F32) for _ in range(NCH)]
    dtot = [cv.view([128, 1], F32) for _ in range(NCH)]
    state = [cv.view([128, NST], F32) for _ in range(NCH)]
    hsall = [cv.view([128, NCORE * NST], F32) for _ in range(NCH)]
    bdj2 = [cv.view([128, NCH * NST], F32) for _ in range(2)]
    bdjD2 = [cv.view([128, NCH], F32) for _ in range(2)]
    dtotall = cv.view([128, NST], F32)
    tails = [cv.view([128, HALO], F32) for _ in range(NCH)]
    tiny = cv.view([128, 1], F32)

    cv.push()
    warena = cv.off
    win_sb = [cv.view([128, 2 * DI], BF16) for _ in range(NCH)]
    wdt_sb = [cv.view([128, DI], BF16) for _ in range(NCH)]
    p1w_end = cv.off
    cv.pop()
    cv.off = warena
    wout_sb = [cv.view([128, DM], BF16) for _ in range(NCH)]
    w1q_sb = [[cv.view([128, HID // 4], BF16) for _ in range(NCH)] for _ in range(2)]
    w2h_sb = [cv.view([128, DM], BF16) for _ in range(8)]
    cv.off = max(cv.off, p1w_end)

    cv.push()
    tarena = cv.off
    xbf8_t = [cv.view([128, TC], BF16) for _ in range(NCH)]
    xsqb_t = [cv.view([128, TC], BF16) for _ in range(4)]
    xln_t = [cv.view([128, TC], BF16) for _ in range(NCH)]
    xp515 = [cv.view([128, HALO + TC], F32) for _ in range(2)]
    cacc_t = [cv.view([128, TC], F32) for _ in range(2)]
    xc_t = [cv.view([128, TC], F32) for _ in range(2)]
    xcb_t = [cv.view([128, TC], BF16) for _ in range(NCH)]
    srb_t = [cv.view([128, TC], BF16) for _ in range(4)]
    g1s_t = [cv.view([128, TC], BF16) for _ in range(4)]
    sp_t = [cv.view([128, TC], F32) for _ in range(4)]
    dlt_t = [cv.view([128, TC], F32) for _ in range(2)]
    cum_t = [cv.view([128, TC], F32) for _ in range(2)]
    sfx_t = [cv.view([128, TC], F32) for _ in range(2)]
    bexp_t = [cv.view([128, TC], BF16) for _ in range(6)]
    wp_t = [cv.view([128, TC], BF16) for _ in range(2)]
    p1_end = cv.off
    cv.pop()
    dltb_t = [cv.view([128, TC], BF16) for _ in range(2)]
    wr_t = [cv.view([128, TC], BF16) for _ in range(2)]
    da_t = [cv.view([128, TC], BF16) for _ in range(8)]
    gbuf = cv.view([128, NST * TC], BF16)
    y_t = cv.view([128, TC], F32)
    gated_t = [[cv.view([128, TC], BF16) for _ in range(NCH)] for _ in range(2)]
    g1r_t = [cv.view([128, TC], BF16) for _ in range(3)]
    g2r_t = [cv.view([128, TC], BF16) for _ in range(3)]
    x1_t = [[cv.view([128, TC], BF16) for _ in range(NCH)] for _ in range(2)]
    x1sq_t = [cv.view([128, TC], BF16) for _ in range(4)]
    xn2_t = [[cv.view([128, TC], BF16) for _ in range(NCH)] for _ in range(2)]
    xr_t = [cv.view([128, TC], F32) for _ in range(2)]
    z_t = [cv.view([128, TC], BF16) for _ in range(8)]
    ysb_t = [cv.view([128, TC], BF16) for _ in range(NCH)]
    yout_t = [cv.view([128, TC], F32) for _ in range(2)]
    lg_t = [cv.view([128, NEXP], F32) for _ in range(4)]
    lgw_t = [cv.view([128, NEXP], F32) for _ in range(4)]
    mx_t = cv.view([128, 1], F32)
    m4_t = [cv.view([128, 1], F32) for _ in range(4)]
    msk_t = [cv.view([128, NEXP], F32) for _ in range(4)]
    me_t = cv.view([NEXP, TC], BF16)
    me_row4 = cv.view([128, 4 * TC], BF16)
    rbst = cv.view([1, NEXP], F32)
    cv.off = max(cv.off, p1_end)
    rA = cv.view([1, TC], F32)
    rB = cv.view([1, TC], F32)
    rC = cv.view([1, TC], F32)
    assert cv.off <= 207 * 1024, f"sbuf overflow {cv.off}"

    psA = nc.alloc_psum_tensor("psA", [128, TC], F32)[:, :]
    psB = nc.alloc_psum_tensor("psB", [128, TC], F32)[:, :]
    psC = nc.alloc_psum_tensor("psC", [128, TC], F32)[:, :]
    psD = nc.alloc_psum_tensor("psD", [128, TC], F32)[:, :]
    psE = nc.alloc_psum_tensor("psE", [128, TC], F32)[:, :]
    psF = nc.alloc_psum_tensor("psF", [128, TC], F32)[:, :]
    PS = {"A": psA, "B": psB, "C": psC, "D": psD, "E": psE, "F": psF}

    c_eps = cv.view([128, 1], F32)
    c_one = cv.view([128, 1], F32)
    c_neg1 = cv.view([128, 1], F32)
    nc.const_aps.register(F32, EPS, c_eps)
    nc.const_aps.register(F32, 1.0, c_one)
    nc.const_aps.register(F32, -1.0, c_neg1)

    pl = Plan(nc, stack)
    bank_free = {}

    def pe_guard(bank):
        b = bank_free.get(bank)
        if b is not None:
            sem, cnt = b
            pl.wait("tensor", sem, cnt)

    def bank_done(bank, sem):
        bank_free[bank] = (sem, pl.val(sem))

    for k in range(NCH):
        r = slice(k * 128, (k + 1) * 128)
        for dst, src in ((convw_t[k], conv_w[r, :]), (convb_t[k], conv_b[r, :]),
                         (bdt_t[k], b_dt[r, :]), (alog_t[k], a_log[r, :]),
                         (dp_t[k], dp_in[r, :]), (btil[k], bp_in[r, :]),
                         (pc_t[k], cp_in[r, :])):
            pl.op("sync", lambda e, d=dst, s=src: e.dma_start(out=d, in_=s), inc="sX", inc_n=16)
    pl.op("sync", lambda e: e.dma_start(out=b1_t, in_=b1c[:, :]), inc="sX", inc_n=16)
    pl.op("sync", lambda e: e.dma_start(out=b2_t, in_=b2c[:, :]), inc="sX", inc_n=16)
    pl.op("sync", lambda e: e.dma_start(out=rbst, in_=rout_b), inc="sX", inc_n=16)
    pl.wait("vector", "sX")
    for k in range(NCH):
        pl.op("vector", lambda e, k=k: e.tensor_mul(bc_t[k], btil[k], pc_t[k]), inc="sDVE")
    pl.op("vector", lambda e: e.memset(ones_k, 1.0), inc="sDVE")
    pl.op("vector", lambda e: e.memset(ones_kf, 1.0), inc="sDVE")
    pl.op("vector", lambda e: e.memset(ones_row, 1.0), inc="sDVE")
    pl.op("vector", lambda e: e.memset(ones_rowb, 1.0), inc="sDVE")
    pl.op("vector", lambda e: e.memset(zero512, 0.0), inc="sDVE")
    pl.op("vector", lambda e: e.memset(c_eps, EPS), inc="sDVE")
    pl.op("vector", lambda e: e.memset(c_one, 1.0), inc="sDVE")
    pl.op("vector", lambda e: e.memset(c_neg1, -1.0), inc="sDVE")
    pl.wait("vector", "sDVE")
    for k in range(NCH):
        pl.op("vector", lambda e, k=k: e.memset(btil[k], 0.0), inc="sDVE")
        pl.op("vector", lambda e, k=k: e.memset(dtot[k], 0.0), inc="sDVE")
        pl.op("vector", lambda e, k=k: e.memset(tails[k], 0.0), inc="sDVE")
    pl.op("gpsimd", lambda e: e.iota(iotaA, [[0, 128]], channel_multiplier=1), inc="sPL")
    pl.op("gpsimd", lambda e: e.iota(iotaB, [[1, 128]], channel_multiplier=0), inc="sPL")
    pl.wait("vector", "sPL")
    pl.op("vector", lambda e: e.tensor_tensor(ident, iotaA, iotaB, ALU.is_equal), inc="sDVE")
    pl.wait("scalar", "sX")
    for k in range(NCH):
        pl.op("scalar", lambda e, k=k: e.activation(pexp_t[k], alog_t[k], AF.Exp), inc="sACT")
    pl.wait("vector", "sACT")
    for k in range(NCH):
        pl.op("vector", lambda e, k=k: e.tensor_scalar(negp_t[k], pexp_t[k], -1.0, None, ALU.mult), inc="sDVE")
    consts_dve = pl.val("sDVE")
    pl.wait("tensor", "sX")
    pl.wait("tensor", "sDVE", consts_dve)
    pl.op("tensor", lambda e: e.matmul(psF[:, 0:NEXP], ones_row, rbst, start=True, stop=True), inc="sPE")
    pl.wait("vector", "sPE")
    pl.op("vector", lambda e: e.tensor_copy(rb_t, psF[:, 0:NEXP]), inc="sDVE")
    bank_done("F", "sDVE")
    srcs = [w1T[i * 128:(i + 1) * 128, h * 2048:(h + 1) * 2048] for i in range(8) for h in range(2)] + \
           [w2T[i * 128:(i + 1) * 128, :] for i in range(32)]
    dsts = [w1b_d[i * 128:(i + 1) * 128, h * 2048:(h + 1) * 2048] for i in range(8) for h in range(2)] + \
           [w2b_d[i * 128:(i + 1) * 128, :] for i in range(32)]
    precast_jobs = list(zip(srcs, dsts))

    def emit_precast(n):
        for _ in range(min(n, len(precast_jobs))):
            s, d = precast_jobs.pop(0)
            pl.op("gpsimd", lambda e, s=s, d=d: e.dma_start(out=d, in_=s), inc="sWC", inc_n=16)

    norm_hist = [None]

    def ln_block(xsrc, width, xlnout, loadw=None):
        loadw = loadw or width
        prev = norm_hist[0]
        for k in range(NCH):
            if prev is not None:
                pl.wait("gpsimd", "sDVE", prev[k])
            pl.op("gpsimd", lambda e, k=k: e.dma_start(out=xbf8_t[k][:, 0:loadw], in_=xsrc(k)), inc="sXB", inc_n=16)
        xw = pl.val("sXB")
        smm = {}
        for k in range(NCH):
            pl.wait("scalar", "sXB", xw)
            if k >= 4:
                pl.wait("scalar", "sPE", smm[k - 4])
            pl.op("scalar", lambda e, k=k: e.activation(xsqb_t[k % 4][:, 0:width], xbf8_t[k][:, 0:width], AF.Square), inc="sACT")
            pl.wait("tensor", "sACT")
            pl.wait("tensor", "sDVE", consts_dve)
            if k < 2:
                pe_guard("E")
                pe_guard("F")
            pl.op("tensor", lambda e, k=k: e.matmul(psE[0:1, 0:width], ones_k, xbf8_t[k][:, 0:width],
                                                    start=(k == 0), stop=(k == NCH - 1)), inc="sPE")
            pl.op("tensor", lambda e, k=k: e.matmul(psF[32:33, 0:width], ones_k, xsqb_t[k % 4][:, 0:width],
                                                    start=(k == 0), stop=(k == NCH - 1)), inc="sPE")
            smm[k] = pl.val("sPE")
        pl.wait("vector", "sPE")
        pl.wait("vector", "sPE2") if False else None
        pl.op("vector", lambda e: e.tensor_scalar(rA[:, 0:width], psE[0:1, 0:width], 1.0 / DM, None, ALU.mult), inc="sDVE")
        pl.op("vector", lambda e: e.tensor_scalar(rB[:, 0:width], psF[32:33, 0:width], 1.0 / DM, None, ALU.mult), inc="sDVE")
        bank_done("E", "sDVE")
        bank_done("F", "sDVE")
        pl.wait("vector", "sDVE")
        pl.op("vector", lambda e: e.tensor_mul(rC[:, 0:width], rA[:, 0:width], rA[:, 0:width]), inc="sDVE")
        pl.wait("vector", "sDVE")
        pl.op("vector", lambda e: e.tensor_tensor(rB[:, 0:width], rB[:, 0:width], rC[:, 0:width], ALU.subtract), inc="sDVE")
        pl.wait("scalar", "sDVE")
        pl.op("scalar", lambda e: e.activation(rB[:, 0:width], rB[:, 0:width], AF.Sqrt, bias=EPS), inc="sACT")
        pl.wait("vector", "sACT")
        pl.op("vector", lambda e: e.reciprocal(rC[:, 0:width], rB[:, 0:width]), inc="sDVE")
        pl.wait("tensor", "sDVE")
        pe_guard("E")
        pe_guard("F")
        pl.op("tensor", lambda e: e.matmul(psE[:, 0:width], ones_row, rA[:, 0:width], start=True, stop=True), inc="sPE")
        pl.op("tensor", lambda e: e.matmul(psF[:, 0:width], ones_row, rC[:, 0:width], start=True, stop=True), inc="sPE")
        nd = []
        for k in range(NCH):
            pl.wait("vector", "sPE")
            pl.op("vector", lambda e, k=k: e.tensor_tensor(xsqb_t[k % 4][:, 0:width], xbf8_t[k][:, 0:width], psE[:, 0:width], ALU.subtract), inc="sDVE")
            pl.wait("vector", "sDVE")
            pl.op("vector", lambda e, k=k: e.tensor_tensor(xlnout[k][:, 0:width], xsqb_t[k % 4][:, 0:width], psF[:, 0:width], ALU.mult), inc="sDVE")
            nd.append(pl.val("sDVE"))
        norm_hist[0] = nd
        bank_done("E", "sDVE")
        bank_done("F", "sDVE")

    def hsrc(k):
        return x_in[k * 128:(k + 1) * 128, 0:TC]

    ln_block(hsrc, HALO, xln_t, loadw=TC)
    for k in range(NCH):
        r = slice(k * 128, (k + 1) * 128)
        pl.op("gpsimd", lambda e, d=win_sb[k], s=winT[r, :]: e.dma_start(out=d, in_=s), inc="sW", inc_n=16)
        pl.op("gpsimd", lambda e, d=wdt_sb[k], s=wdtT[r, :]: e.dma_start(out=d, in_=s), inc="sW", inc_n=16)
        pl.op("gpsimd", lambda e, d=rout_sb[k], s=routT[r, :]: e.dma_start(out=d, in_=s), inc="sW", inc_n=16)
    w_loaded = pl.val("sW")
    emit_precast(8)
    pl.wait("tensor", "sW", w_loaded)
    for m in range(NCH):
        pl.wait("tensor", "sDVE")
        pe_guard("A")
        for k in range(NCH):
            pl.op("tensor", lambda e, m=m, k=k: e.matmul(
                psA[:, 0:HALO], win_sb[k][:, m * 128:(m + 1) * 128], xln_t[k][:, 0:HALO],
                start=(k == 0), stop=(k == NCH - 1)), inc="sPE")
        pl.wait("vector", "sPE")
        pl.op("vector", lambda e, m=m: e.tensor_copy(tails[m], psA[:, 0:HALO]), inc="sDVE")
        bank_done("A", "sDVE")

    def z_piece(c, j):
        ccols = slice(c * TC, (c + 1) * TC)
        m0, m1 = 8 + 2 * j, 8 + 2 * j + 1
        pe_guard("E")
        pe_guard("F")
        for k in range(NCH):
            pl.op("tensor", lambda e, k=k, m0=m0: e.matmul(
                psE, win_sb[k][:, m0 * 128:(m0 + 1) * 128], xln_t[k],
                start=(k == 0), stop=(k == NCH - 1)), inc="sPE")
            pl.op("tensor", lambda e, k=k, m1=m1: e.matmul(
                psF, win_sb[k][:, m1 * 128:(m1 + 1) * 128], xln_t[k],
                start=(k == 0), stop=(k == NCH - 1)), inc="sPE")
        for half, psh in ((0, "E"), (1, "F")):
            k2 = 2 * j + half
            sl = k2 % 4
            pl.wait("scalar", "sPE")
            pl.wait("scalar", f"rZ{sl}")
            pl.op("scalar", lambda e, k2=k2, sl=sl, p=PS[psh]: e.activation(srb_t[sl], p, AF.Silu), inc="sACT")
            bank_done(psh, "sACT")
            pl.wait("sync", "sACT")
            pl.op("sync", lambda e, k2=k2, sl=sl, ccols=ccols: e.dma_start(
                out=g2_d[k2 * 128:(k2 + 1) * 128, ccols], in_=srb_t[sl]), inc=f"rZ{sl}", inc_n=16)
            pl.wait("vector", "sACT")
            pl.wait("vector", f"rZ{sl}")
            pl.op("vector", lambda e, k2=k2, sl=sl: e.tensor_scalar(g1s_t[sl], xcb_t[k2], dp_t[k2][:, 0:1], None, ALU.mult), inc="sDVE")
            pl.wait("vector", "sDVE")
            pl.op("vector", lambda e, k2=k2, sl=sl: e.tensor_mul(g1s_t[sl], g1s_t[sl], srb_t[sl]), inc="sDVE")
            pl.wait("sync", "sDVE")
            pl.op("sync", lambda e, k2=k2, sl=sl, ccols=ccols: e.dma_start(
                out=g1_d[k2 * 128:(k2 + 1) * 128, ccols], in_=g1s_t[sl]), inc=f"rZ{sl}", inc_n=16)

    def z_section(c):
        for j in range(4):
            z_piece(c, j)

    dlt_done = [0]
    for c in range(NCHUNK):
        x0 = HALO + c * TC

        def csrc(k, x0=x0):
            return x_in[k * 128:(k + 1) * 128, x0:x0 + TC]

        ln_block(csrc, TC, xln_t)
        emit_precast(10)

        xln_ready = pl.val("sDVE")
        for m in range(8):
            if m == 0:
                pl.wait("tensor", "sDVE", xln_ready)
            ps = "A" if m % 2 == 0 else "B"
            pe_guard(ps)
            for k in range(NCH):
                pl.op("tensor", lambda e, m=m, k=k, p=PS[ps]: e.matmul(
                    p, win_sb[k][:, m * 128:(m + 1) * 128], xln_t[k],
                    start=(k == 0), stop=(k == NCH - 1)), inc="sPE")
            if True:
                pl.wait("vector", "sPE")
                pl.wait("vector", "sACT")
                pl.op("vector", lambda e, m=m, p=PS[ps]: e.tensor_copy(xp515[m % 2][:, HALO:], p), inc="sDVE")
                bank_done(ps, "sDVE")
                pl.op("vector", lambda e, m=m: e.tensor_copy(xp515[m % 2][:, 0:HALO], tails[m]), inc="sDVE")
                pl.wait("vector", "sDVE")
                pl.op("vector", lambda e, m=m: e.tensor_scalar(
                    cacc_t[m % 2], xp515[m % 2][:, 0:TC], convw_t[m][:, 0:1], None, ALU.mult), inc="sDVE")
                for j in range(1, 4):
                    pl.wait("vector", "sDVE")
                    pl.op("vector", lambda e, m=m, j=j: e.scalar_tensor_tensor(
                        cacc_t[m % 2], xp515[m % 2][:, j:j + TC], convw_t[m][:, j:j + 1],
                        cacc_t[m % 2], ALU.mult, ALU.add), inc="sDVE")
                pl.op("vector", lambda e, m=m: e.tensor_copy(tails[m], xp515[m % 2][:, TC:TC + HALO]), inc="sDVE")
                pl.wait("scalar", "sDVE")
                pl.op("scalar", lambda e, m=m: e.activation(xc_t[m % 2], cacc_t[m % 2], AF.Silu, bias=convb_t[m][:, 0:1]), inc="sACT")
                pl.wait("vector", "sACT")
                pl.op("vector", lambda e, m=m: e.tensor_copy(xcb_t[m], xc_t[m % 2]), inc="sDVE")
        xcb_ready = pl.val("sDVE")
        for m in range(NCH):
            if m == 0:
                pl.wait("tensor", "sDVE", xcb_ready)
            ps = "A" if m % 2 == 0 else "B"
            pe_guard(ps)
            for k in range(NCH):
                pl.op("tensor", lambda e, m=m, k=k, p=PS[ps]: e.matmul(
                    p, wdt_sb[k][:, m * 128:(m + 1) * 128], xcb_t[k],
                    start=(k == 0), stop=(k == NCH - 1)), inc="sPE")
            pl.wait("scalar", "sPE")
            if dlt_done[0]:
                pl.wait("scalar", "sDVE", dlt_done[0])
            pl.op("scalar", lambda e, m=m, p=PS[ps]: e.activation(sp_t[0], p, AF.Relu, bias=bdt_t[m][:, 0:1]), inc="sACT")
            pl.op("scalar", lambda e, m=m, p=PS[ps]: e.activation(sp_t[1], p, AF.Abs, bias=bdt_t[m][:, 0:1]), inc="sACT")
            bank_done(ps, "sACT")
            pl.wait("scalar", "sACT")
            pl.op("scalar", lambda e: e.activation(sp_t[2], sp_t[1], AF.Exp, scale=-1.0), inc="sACT")
            pl.wait("scalar", "sACT")
            pl.op("scalar", lambda e: e.activation(sp_t[3], sp_t[2], AF.Ln, bias=1.0), inc="sACT")
            pl.wait("vector", "sACT")
            pl.wait("vector", "sW")
            pl.op("vector", lambda e, m=m: e.tensor_add(dlt_t[m % 2], sp_t[0], sp_t[3]), inc="sDVE")
            dlt_done[0] = pl.val("sDVE")
            pl.wait("gpsimd", "sDVE")
            pl.op("gpsimd", lambda e, m=m, c=c: e.dma_start(
                out=delta_d[m * 128:(m + 1) * 128, c * TC:(c + 1) * TC], in_=dlt_t[m % 2]), inc="sW", inc_n=16)
            pl.wait("vector", "sDVE")
            pl.wait("vector", "sS")
            pl.op("vector", lambda e, m=m, c=c: e.tensor_mul(
                wp_t[m % 2], dlt_t[m % 2], xcb_t[m]), inc="sDVE")
            pl.wait("sync", "sDVE")
            pl.op("sync", lambda e, m=m, c=c: e.dma_start(
                out=w_d[m * 128:(m + 1) * 128, c * TC:(c + 1) * TC], in_=wp_t[m % 2]), inc="sS", inc_n=16)
            pl.op("vector", lambda e, m=m: e.tensor_tensor_scan(
                cum_t[m % 2], dlt_t[m % 2], zero512, 0.0, ALU.add, ALU.add), inc="sDVE")
            pl.wait("vector", "sDVE")
            pl.op("vector", lambda e, m=m: e.tensor_scalar(
                sfx_t[m % 2], cum_t[m % 2], cum_t[m % 2][:, TC - 1:TC], None, ALU.subtract), inc="sDVE")
            pl.wait("scalar", "sDVE")
            pl.op("scalar", lambda e, m=m: e.activation(pc_t[m], negp_t[m], AF.Exp, scale=cum_t[m % 2][:, TC - 1:TC]), inc="sACT")
            for n in range(NST):
                b = n % 6
                pl.wait("scalar", "rBF", max(0, pl.val("rBF") - 5))
                pl.op("scalar", lambda e, m=m, n=n, b=b: e.activation(
                    bexp_t[b], sfx_t[m % 2], AF.Exp, scale=pexp_t[m][:, n:n + 1]), inc="rB")
                pl.wait("vector", "rB")
                pl.op("vector", lambda e, m=m, n=n, b=b: e.scalar_tensor_tensor(
                    bexp_t[b], bexp_t[b], 1.0, wp_t[m % 2], ALU.mult, ALU.mult,
                    accum_out=bccum[:, n:n + 1]), inc="rBF")
            pl.wait("vector", "sACT")
            pl.wait("vector", "rBF")
            pl.op("vector", lambda e, m=m: e.tensor_mul(btil[m], btil[m], pc_t[m]), inc="sDVE")
            pl.wait("vector", "sDVE")
            pl.op("vector", lambda e, m=m: e.tensor_add(btil[m], btil[m], bccum), inc="sDVE")
            pl.op("vector", lambda e, m=m: e.tensor_tensor(dtot[m], dtot[m], cum_t[m % 2][:, TC - 1:TC], ALU.add), inc="sDVE")
        if c < NCHUNK - 1:
            z_section(c)

    emit_precast(99)
    wcast_done = pl.val("sWC")
    pl.wait("vector", "sDVE")
    pl.op("vector", lambda e: e.memset(dtotall, 0.0), inc="sDVE")
    pl.wait("vector", "sDVE")
    for m in range(NCH):
        pl.op("vector", lambda e, m=m: e.tensor_copy(dtotall[:, m:m + 1], dtot[m]),
              inc="sDVE" if m == NCH - 1 else None)
    pl.wait("sync", "sDVE")
    pl.op("sync", lambda e: e.dma_start(out=cin_d[DI:DI + 128, 0:NST], in_=dtotall), inc="sS", inc_n=16)
    for m in range(NCH):
        r = slice(m * 128, (m + 1) * 128)
        pl.op("sync", lambda e, m=m, r=r: e.dma_start(out=cin_d[r, 0:NST], in_=btil[m]), inc="sS", inc_n=16)
        if debug:
            pl.op("sync", lambda e, m=m, r=r: e.dma_start(out=dbg_b[r, 0:NST], in_=btil[m]), inc="sS", inc_n=16)
    pl.wait("gpsimd", "sS")
    pl.op("gpsimd", lambda e: e.collective_compute(
        "AllGather", ALU.bypass, replica_groups=[list(range(NCORE))],
        ins=[cin_d], outs=[cout_d]), inc="sCC")

    z_section(NCHUNK - 1)
    g12_done = [(f"rZ{s}", pl.val(f"rZ{s}")) for s in range(4)]

    pl.wait("vector", "sDVE")
    for m in range(NCH):
        pl.op("vector", lambda e, m=m: e.memset(hsall[m][:, 0:NST], 0.0), inc="sDVE")
    pl.wait("sync", "sCC")
    chain_av, chain_dv = {}, {}
    for j in range(NCORE - 1):
        b2i = j % 2
        base = j * (DI + 128)
        srcB = cout_d[base:base + DI, :].rearrange("(c p) f -> p c f", p=128)
        bdj3 = bdj2[b2i].rearrange("p (c f) -> p c f", c=NCH)
        if j >= 2:
            pl.wait("sync", "sACT", chain_av[j - 2])
            pl.wait("sync", "sDVE", chain_dv[j - 2])
        pl.op("sync", lambda e, s=srcB, d=bdj3: e.dma_start(out=d, in_=s), inc=f"sXT{b2i}", inc_n=16)
        pl.op("sync", lambda e, s=cout_d[base + DI:base + DI + 128, 0:NCH], d=bdjD2[b2i]: e.dma_start(out=d, in_=s), inc=f"sXT{b2i}", inc_n=16)
        pl.wait("scalar", f"sXT{b2i}")
        pl.wait("scalar", "sDVE")
        for m in range(NCH):
            pl.op("scalar", lambda e, m=m, b2i=b2i: e.activation(
                pc_t[m], negp_t[m], AF.Exp,
                scale=bdjD2[b2i][:, m:m + 1]), inc="sACT")
        chain_av[j] = pl.val("sACT")
        pl.wait("vector", "sACT")
        for m in range(NCH):
            pl.op("vector", lambda e, m=m, j=j: e.tensor_mul(
                hsall[m][:, (j + 1) * NST:(j + 2) * NST], hsall[m][:, j * NST:(j + 1) * NST], pc_t[m]), inc="sDVE")
            pl.wait("vector", "sDVE")
            pl.op("vector", lambda e, m=m, j=j, b2i=b2i: e.tensor_tensor(
                hsall[m][:, (j + 1) * NST:(j + 2) * NST], hsall[m][:, (j + 1) * NST:(j + 2) * NST],
                bdj2[b2i][:, m * NST: (m + 1) * NST], ALU.add), inc="sDVE")
        chain_dv[j] = pl.val("sDVE")

    sdve_h = pl.sem("sDVE")

    def emit_switch(v):
        for arm in v.switch_core_id(n=128):
            lg = arm.logical
            sel = (lg % NCORE) if lg is not None else 0
            v.memset(tiny, 1000.0 + sel)
            for m in range(NCH):
                ins = v.tensor_copy(state[m], hsall[m][:, sel * NST:(sel + 1) * NST])
                if m == NCH - 1:
                    ins.then_inc(sdve_h, 1)

    pl.wait("vector", "sDVE")
    if debug:
        pl.wait("sync", "sDVE")
        pl.op("sync", lambda e: e.dma_start(out=dbg_hsall, in_=hsall[0]), inc="sS", inc_n=16)
    pl.op("vector", emit_switch)
    pl.cnt["sDVE"] += 1
    svsel = pl.val("sDVE")
    if debug:
        pl.wait("sync", "sDVE", svsel)
        for m in range(NCH):
            r = slice(m * 128, (m + 1) * 128)
            pl.op("sync", lambda e, m=m, r=r: e.dma_start(out=dbg_hs[r, :], in_=state[m]), inc="sS", inc_n=16)
        pl.op("sync", lambda e: e.dma_start(out=dbg_bdj[0:128, 0:1], in_=tiny, allow_slow_non_contiguous=True), inc="sS", inc_n=16)

    p1_pe = pl.val("sPE")
    pl.wait("gpsimd", "sPE", p1_pe)
    for k in range(NCH):
        r = slice(k * 128, (k + 1) * 128)
        pl.op("gpsimd", lambda e, d=wout_sb[k], s=woutT[r, :]: e.dma_start(out=d, in_=s), inc="sW", inc_n=16)
    wout_loaded = pl.val("sW")
    pl.wait("sync", "sW")
    pl.wait("sync", "sS")
    pl.wait("sync", "sWC", wcast_done)
    pl.wait("gpsimd", "sWC", wcast_done)

    gate_all = {}
    wout_pe = {}
    xn2_ok = {}
    lnrt_pe = {}
    gate_vals = {}
    x1_vals = {}
    ffn_h_done = {}
    ffn_red_done = {}
    ffn_z_act = {}
    ffn_w2_ready = {}
    me_store_val = {}
    last_tree = [pl.val("sDVE")]

    def scan_block(t, m):
        tc0 = t * TC
        d = m % 2
        pl.wait("sync", "rA", max(0, pl.val("rA") - 4))
        pl.op("sync", lambda e, m=m, d=d, tc0=tc0: e.dma_start(
            out=dltb_t[d], in_=delta_d[m * 128:(m + 1) * 128, tc0:tc0 + TC]), inc=f"rDL{d}", inc_n=16)
        pl.wait("sync", "rAF", max(0, pl.val("rAF") - 4))
        pl.op("sync", lambda e, m=m, d=d, tc0=tc0: e.dma_start(
            out=wr_t[d], in_=w_d[m * 128:(m + 1) * 128, tc0:tc0 + TC]), inc=f"rWR{d}", inc_n=16)
        pl.wait("vector", "sDVE", last_tree[0])
        if t == 0 and m == 0:
            pl.wait("vector", "sDVE", svsel)
        pl.wait("vector", f"rWR{d}")
        for g in range(4):
            pl.wait("scalar", "rAF", max(0, pl.val("rAF") - 1))
            if g == 0:
                pl.wait("scalar", f"rDL{d}")
            for i in range(4):
                n = g * 4 + i
                a = n % 8
                pl.op("scalar", lambda e, m=m, n=n, a=a, d=d: e.activation(
                    da_t[a], dltb_t[d], AF.Exp, scale=negp_t[m][:, n:n + 1]),
                    inc="rA" if i == 3 else None)
            pl.wait("vector", "rA")
            for i in range(4):
                n = g * 4 + i
                a = n % 8
                pl.op("vector", lambda e, m=m, n=n, a=a, d=d: e.tensor_tensor_scan(
                    gbuf[:, n * TC:(n + 1) * TC], da_t[a], wr_t[d],
                    state[m][:, n:n + 1], ALU.mult, ALU.add),
                    inc="rAF" if i == 3 else None)
        pl.wait("vector", "rAF")
        gtail = gbuf.rearrange("p (n t) -> p n t", n=NST)[:, :, TC - 1:TC]
        pl.op("vector", lambda e, m=m, g=gtail: e.tensor_copy(
            state[m].unsqueeze(2), g), inc="sDVE")
        pl.wait("vector", "sDVE")
        for n in range(NST):
            pl.op("vector", lambda e, m=m, n=n: e.tensor_scalar(
                gbuf[:, n * TC:(n + 1) * TC], gbuf[:, n * TC:(n + 1) * TC],
                bc_t[m][:, n:n + 1], None, ALU.mult),
                inc="sDVE" if n == NST - 1 else None)
        pl.wait("vector", "sDVE")
        for n in range(8):
            pl.op("vector", lambda e, n=n: e.tensor_add(
                gbuf[:, (2 * n) * TC:(2 * n + 1) * TC],
                gbuf[:, (2 * n) * TC:(2 * n + 1) * TC],
                gbuf[:, (2 * n + 1) * TC:(2 * n + 2) * TC]),
                inc="sDVE" if n == 7 else None)
        pl.wait("vector", "sDVE")
        for n in range(4):
            pl.op("vector", lambda e, n=n: e.tensor_add(
                gbuf[:, (4 * n) * TC:(4 * n + 1) * TC],
                gbuf[:, (4 * n) * TC:(4 * n + 1) * TC],
                gbuf[:, (4 * n + 2) * TC:(4 * n + 3) * TC]),
                inc="sDVE" if n == 3 else None)
        pl.wait("vector", "sDVE")
        for n in range(2):
            pl.op("vector", lambda e, n=n: e.tensor_add(
                gbuf[:, (8 * n) * TC:(8 * n + 1) * TC],
                gbuf[:, (8 * n) * TC:(8 * n + 1) * TC],
                gbuf[:, (8 * n + 4) * TC:(8 * n + 5) * TC]),
                inc="sDVE" if n == 1 else None)
        pl.wait("vector", "sDVE")
        pl.op("vector", lambda e: e.tensor_tensor(
            y_t, gbuf[:, 0:TC], gbuf[:, 8 * TC:9 * TC], ALU.add), inc="sDVE")
        g4 = (t * NCH + m) % 3
        gv = gate_vals.get((t, m - 3), 0) if m >= 3 else gate_vals.get((t - 1, m + 5), 0)
        if gv:
            pl.wait("sync", "sDVE", gv)
        if t == 0 and m < 3:
            for gs, gv2 in g12_done:
                pl.wait("sync", gs, gv2)
        pl.op("sync", lambda e, m=m, g4=g4, tc0=tc0: e.dma_start(
            out=g2r_t[g4], in_=g2_d[m * 128:(m + 1) * 128, tc0:tc0 + TC]), inc=f"rGG{g4}", inc_n=16)
        pl.op("sync", lambda e, m=m, g4=g4, tc0=tc0: e.dma_start(
            out=g1r_t[g4], in_=g1_d[m * 128:(m + 1) * 128, tc0:tc0 + TC]), inc=f"rGG{g4}", inc_n=16)
        pl.wait("vector", f"rGG{g4}")
        if t >= 2:
            pl.wait("vector", "sPE", wout_pe.get(t - 2, 0))
        pl.wait("vector", "sDVE")
        pl.op("vector", lambda e, m=m, g4=g4: e.tensor_mul(y_t, y_t, g2r_t[g4]), inc="sDVE")
        pl.wait("vector", "sDVE")
        pl.op("vector", lambda e, m=m, g4=g4, t=t: e.tensor_tensor(
            gated_t[t % 2][m], y_t, g1r_t[g4], ALU.add), inc="sDVE")
        gate_vals[(t, m)] = pl.val("sDVE")
        last_tree[0] = pl.val("sDVE")
        if debug:
            pl.wait("sync", "sDVE")
            pl.op("sync", lambda e, m=m, t=t, tc0=tc0: e.dma_start(
                out=dbg_y[m * 128:(m + 1) * 128, tc0:tc0 + TC], in_=gated_t[t % 2][m]), inc="sS", inc_n=16)

    def wout_piece(t, mt):
        tc0 = t * TC
        ps = "A" if mt % 2 == 0 else "B"
        if mt == 0:
            pl.wait("tensor", "sW", wout_loaded)
            pl.wait("tensor", "sDVE", gate_all[t])
        pe_guard(ps)
        for k in range(NCH):
            pl.op("tensor", lambda e, mt=mt, k=k, p=PS[ps], t=t: e.matmul(
                p, wout_sb[k][:, mt * 128:(mt + 1) * 128], gated_t[t % 2][k],
                start=(k == 0), stop=(k == NCH - 1)), inc="sPE")
        xv = x1_vals.get((t, mt - 2), 0) if mt >= 2 else x1_vals.get((t - 1, mt + 6), 0)
        if xv:
            pl.wait("sync", "sDVE", xv)
        pl.op("sync", lambda e, mt=mt, tc0=tc0: e.dma_start(
            out=xr_t[mt % 2], in_=x_in[mt * 128:(mt + 1) * 128, HALO + tc0:HALO + tc0 + TC]), inc=f"rXR{mt % 2}", inc_n=16)
        pl.wait("vector", f"rXR{mt % 2}")
        pl.wait("vector", "sPE")
        pl.op("vector", lambda e, mt=mt, p=PS[ps], t=t: e.tensor_tensor(
            x1_t[t % 2][mt], p, xr_t[mt % 2], ALU.add), inc="sDVE")
        x1_vals[(t, mt)] = pl.val("sDVE")
        bank_done(ps, "sDVE")
        if debug:
            pl.wait("sync", "sDVE")
            pl.op("sync", lambda e, mt=mt, tc0=tc0, t=t: e.dma_start(
                out=dbg_x1[mt * 128:(mt + 1) * 128, tc0:tc0 + TC], in_=x1_t[t % 2][mt]), inc="sS", inc_n=16)

    ln2_smm = [{}, {}]

    def ln2_router(t):
        tc0 = t * TC
        tb2 = t % 2
        for k in range(NCH):
            pl.wait("scalar", "sDVE")
            sv = ln2_smm[0].get(k - 4) if k >= 4 else ln2_smm[1].get(k + 4)
            if sv:
                pl.wait("scalar", "sPE", sv)
            pl.op("scalar", lambda e, k=k, tb2=tb2: e.activation(x1sq_t[k % 4], x1_t[tb2][k], AF.Square), inc="sACT")
            pl.wait("tensor", "sACT")
            if k < 2:
                pe_guard("A")
                pe_guard("B")
            pl.op("tensor", lambda e, k=k, tb2=tb2: e.matmul(psA[0:1, :], ones_k, x1_t[tb2][k],
                                                    start=(k == 0), stop=(k == NCH - 1)), inc="sPE")
            pl.op("tensor", lambda e, k=k: e.matmul(psB[32:33, :], ones_k, x1sq_t[k % 4],
                                                    start=(k == 0), stop=(k == NCH - 1)), inc="sPE")
            ln2_smm[0][k] = pl.val("sPE")
        ln2_smm[1] = ln2_smm[0]
        ln2_smm[0] = {}
        pl.wait("vector", "sPE")
        pl.op("vector", lambda e: e.tensor_scalar(rA, psA[0:1, :], 1.0 / DM, None, ALU.mult), inc="sDVE")
        pl.op("vector", lambda e: e.tensor_scalar(rB, psB[32:33, :], 1.0 / DM, None, ALU.mult), inc="sDVE")
        bank_done("A", "sDVE")
        bank_done("B", "sDVE")
        pl.wait("vector", "sDVE")
        pl.op("vector", lambda e: e.tensor_mul(rC, rA, rA), inc="sDVE")
        pl.wait("vector", "sDVE")
        pl.op("vector", lambda e: e.tensor_tensor(rB, rB, rC, ALU.subtract), inc="sDVE")
        pl.wait("scalar", "sDVE")
        pl.op("scalar", lambda e: e.activation(rB, rB, AF.Sqrt, bias=EPS), inc="sACT")
        pl.wait("vector", "sACT")
        pl.op("vector", lambda e: e.reciprocal(rC, rB), inc="sDVE")
        pl.wait("tensor", "sDVE")
        pe_guard("A")
        pe_guard("B")
        pl.op("tensor", lambda e: e.matmul(psA, ones_row, rA, start=True, stop=True), inc="sPE")
        pl.op("tensor", lambda e: e.matmul(psB, ones_row, rC, start=True, stop=True), inc="sPE")
        for k in range(NCH):
            pl.wait("vector", "sPE")
            pl.wait("vector", "sS")
            pl.op("vector", lambda e, k=k, tb2=tb2: e.tensor_tensor(yout_t[k % 2], x1_t[tb2][k], psA, ALU.subtract), inc="sDVE")
            pl.wait("vector", "sDVE")
            pl.op("vector", lambda e, k=k, tb2=tb2: e.tensor_tensor(xn2_t[tb2][k], yout_t[k % 2], psB, ALU.mult), inc="sDVE")
        bank_done("A", "sDVE")
        bank_done("B", "sDVE")
        xn2_ok[t] = pl.val("sDVE")
        pl.wait("tensor", "sDVE", xn2_ok[t])
        pe_guard("A")
        for tb in range(4):
            for k in range(NCH):
                pl.op("tensor", lambda e, tb=tb, k=k, tb2=tb2: e.matmul(
                    psA[:, tb * NEXP:(tb + 1) * NEXP], xn2_t[tb2][k][:, tb * 128:(tb + 1) * 128],
                    rout_sb[k], start=(k == 0), stop=(k == NCH - 1)), inc="sPE")
        pl.wait("vector", "sPE")
        for tb in range(4):
            pl.op("vector", lambda e, tb=tb: e.tensor_tensor(
                lg_t[tb], psA[:, tb * NEXP:(tb + 1) * NEXP], rb_t, ALU.add), inc="sDVE")
        bank_done("A", "sDVE")
        for tb in range(4):
            pl.wait("vector", "sDVE")
            pl.op("vector", lambda e, tb=tb: e.tensor_copy(lgw_t[tb], lg_t[tb]), inc="sDVE")
            for it in range(TOPK - 1):
                pl.wait("vector", "sDVE")
                pl.op("vector", lambda e, tb=tb: e.reduce_max(mx_t, lgw_t[tb], axis=mybir.AxisListType.X), inc="sDVE")
                pl.wait("vector", "sDVE")
                pl.op("vector", lambda e, tb=tb: e.tensor_scalar(
                    msk_t[tb], lgw_t[tb], mx_t[:, 0:1], None, ALU.is_ge), inc="sDVE")
                pl.wait("vector", "sDVE")
                pl.op("vector", lambda e, tb=tb: e.scalar_tensor_tensor(
                    lgw_t[tb], msk_t[tb], -BIG, lgw_t[tb], ALU.mult, ALU.add), inc="sDVE")
            pl.wait("vector", "sDVE")
            pl.op("vector", lambda e, tb=tb: e.reduce_max(m4_t[tb], lgw_t[tb], axis=mybir.AxisListType.X), inc="sDVE")
            pl.wait("vector", "sDVE")
            pl.op("vector", lambda e, tb=tb: e.tensor_scalar(
                msk_t[tb], lg_t[tb], m4_t[tb][:, 0:1], None, ALU.is_ge), inc="sDVE")
        if debug:
            pl.wait("sync", "sDVE")
            for tb in range(4):
                pl.op("sync", lambda e, tb=tb, tc0=tc0: e.dma_start(
                    out=dbg_msk[tc0 + tb * 128: tc0 + (tb + 1) * 128, :], in_=msk_t[tb]), inc="sS", inc_n=16)
            for k in range(NCH):
                pl.op("sync", lambda e, k=k, tc0=tc0, tb2=tb2: e.dma_start(
                    out=dbg_xn2[k * 128:(k + 1) * 128, tc0:tc0 + TC], in_=xn2_t[tb2][k]), inc="sS", inc_n=16)
            for tb in range(4):
                pl.op("sync", lambda e, tb=tb, tc0=tc0: e.dma_start(
                    out=dbg_lg[tc0 + tb * 128: tc0 + (tb + 1) * 128, :], in_=lg_t[tb]), inc="sS", inc_n=16)
        pl.wait("tensor", "sDVE")
        pe_guard("B")
        for tb in range(4):
            pl.op("tensor", lambda e, tb=tb: e.transpose(
                psB[0:NEXP, tb * 128:(tb + 1) * 128], msk_t[tb], ident), inc="sPE")
        pl.wait("vector", "sPE")
        pl.op("vector", lambda e: e.tensor_scalar(me_t, psB[0:NEXP, :], BIG, -BIG, ALU.mult, ALU.add), inc="sDVE")
        bank_done("B", "sDVE")
        pl.wait("sync", "sDVE")
        pl.op("sync", lambda e, tb2=tb2: e.dma_start(out=me_d[tb2], in_=me_t), inc="sMD", inc_n=16)
        me_store_val[t] = pl.val("sMD")
        lnrt_pe[t] = pl.val("sPE")

    def ffn_step(t, s):
        tc0 = t * TC
        tb2 = t % 2
        q = s // 2
        if s % 2 == 0:
            wv = ffn_h_done.get((t, q - 2), 0) if q >= 2 else ffn_h_done.get((t - 1, q + 2), 0)
            if wv:
                pl.wait("gpsimd", "sPE", wv)
            for k in range(NCH):
                pl.op("gpsimd", lambda e, q=q, k=k: e.dma_start(
                    out=w1q_sb[q % 2][k], in_=w1b_d[k * 128:(k + 1) * 128, q * (HID // 4):(q + 1) * (HID // 4)]), inc=f"rW1{q % 2}", inc_n=16)
            rv = ffn_red_done.get((t, q - 1), 0) if q >= 1 else ffn_red_done.get((t - 1, 3), 0)
            if rv:
                pl.wait("gpsimd", "sPE", rv)
            for hs in range(8):
                pl.op("gpsimd", lambda e, q=q, hs=hs: e.dma_start(
                    out=w2h_sb[hs], in_=w2b_d[q * (HID // 4) + hs * 128: q * (HID // 4) + (hs + 1) * 128, :]), inc="rW2", inc_n=16)
            pl.wait("gpsimd", "sMD", me_store_val[t])
            for i in range(4):
                pl.op("gpsimd", lambda e, q=q, i=i, tb2=tb2: e.dma_start(
                    out=me_row4[0:1, i * TC:(i + 1) * TC], in_=me_d[tb2][q * 4 + i:q * 4 + i + 1, :]), inc="sME", inc_n=16)
            w1_ready = pl.val(f"rW1{q % 2}")
            ffn_w2_ready[(t, q)] = pl.val("rW2")
            pen_ready = pl.val("sME")
            if rv:
                pl.wait("scalar", "sPE", rv)
            for hs in range(8):
                hb = "C" if hs % 2 == 0 else "D"
                pl.wait("tensor", f"rW1{q % 2}", w1_ready)
                pl.wait("tensor", "sME", pen_ready)
                pl.wait("tensor", "sDVE", xn2_ok[t])
                pe_guard(hb)
                for k in range(NCH):
                    pl.op("tensor", lambda e, k=k, hs=hs, q=q, p=PS[hb], tb2=tb2: e.matmul(
                        p, w1q_sb[q % 2][k][:, hs * 128:(hs + 1) * 128], xn2_t[tb2][k],
                        start=(k == 0), stop=False), inc="sPE")
                pl.op("tensor", lambda e, hs=hs, p=PS[hb]: e.matmul(
                    p, ones_rowb, me_row4[0:1, (hs // 2) * TC:(hs // 2 + 1) * TC],
                    start=False, stop=True), inc="sPE")
                pl.wait("scalar", "sPE")
                pl.op("scalar", lambda e, q=q, hs=hs, p=PS[hb]: e.activation(
                    z_t[hs], p, AF.Relu, bias=b1_t[:, q * 8 + hs: q * 8 + hs + 1]), inc="sACT")
                bank_done(hb, "sACT")
            ffn_h_done[(t, q)] = pl.val("sPE")
            ffn_z_act[(t, q)] = pl.val("sACT")
        else:
            for ot in range(NCH):
                yb = "E" if ot % 2 == 0 else "F"
                pe_guard(yb)
                if ot == 0:
                    pl.wait("tensor", "sACT", ffn_z_act[(t, q)])
                    pl.wait("tensor", "rW2", ffn_w2_ready[(t, q)])
                for hs in range(8):
                    pl.op("tensor", lambda e, ot=ot, hs=hs, p=PS[yb]: e.matmul(
                        p, w2h_sb[hs][:, ot * 128:(ot + 1) * 128], z_t[hs],
                        start=(hs == 0), stop=(hs == 7)), inc="sPE")
                pl.wait("vector", "sPE")
                if q == 0:
                    pl.op("vector", lambda e, ot=ot, p=PS[yb], tb2=tb2: e.scalar_tensor_tensor(
                        ysb_t[ot], p, b2_t[:, ot:ot + 1], x1_t[tb2][ot], ALU.add, ALU.add), inc="sDVE")
                elif q < 3:
                    pl.op("vector", lambda e, ot=ot, p=PS[yb]: e.tensor_tensor(
                        ysb_t[ot], p, ysb_t[ot], ALU.add), inc="sDVE")
                else:
                    pl.wait("vector", "sS")
                    pl.op("vector", lambda e, ot=ot, p=PS[yb]: e.tensor_tensor(
                        yout_t[ot % 2], p, ysb_t[ot], ALU.add), inc="sDVE")
                    pl.wait("sync", "sDVE")
                    pl.op("sync", lambda e, ot=ot, tc0=tc0: e.dma_start(
                        out=out_d[ot * 128:(ot + 1) * 128, tc0:tc0 + TC], in_=yout_t[ot % 2]), inc="sS", inc_n=16)
                bank_done(yb, "sDVE")
            ffn_red_done[(t, q)] = pl.val("sPE")

    for t in range(NCHUNK + 1):
        for m in range(NCH):
            if t < NCHUNK:
                scan_block(t, m)
            if m == 0:
                if t >= 1:
                    for mt in range(NCH):
                        wout_piece(t - 1, mt)
                    wout_pe[t - 1] = pl.val("sPE")
                if t >= 2:
                    ffn_step(t - 2, 6)
            elif m == 1:
                if t >= 1:
                    ln2_router(t - 1)
                if t >= 2:
                    ffn_step(t - 2, 7)
            else:
                if t >= 1:
                    ffn_step(t - 1, m - 2)
        if t < NCHUNK:
            gate_all[t] = gate_vals[(t, NCH - 1)]
    ffn_step(NCHUNK - 1, 6)
    ffn_step(NCHUNK - 1, 7)

    pl.wait("sync", "sS")
    pl.wait("sync", "sW")
    pl.wait("sync", "sX")

    with nc.Block() as block:
        @block.sync
        def _(e):
            for f in pl.ops["sync"]:
                f(e)

        @block.vector
        def _(e):
            for f in pl.ops["vector"]:
                f(e)

        @block.scalar
        def _(e):
            for f in pl.ops["scalar"]:
                f(e)

        @block.tensor
        def _(e):
            for f in pl.ops["tensor"]:
                f(e)

        @block.gpsimd
        def _(e):
            for f in pl.ops["gpsimd"]:
                f(e)

    stack.close()
    return nc


def kernel(**inputs):
    x = np.asarray(inputs["L0_attn_x"])
    xT = np.ascontiguousarray(x[0].T)

    shared = dict(
        winT=np.ascontiguousarray(np.asarray(inputs["L0_attn_W_in"]).T),
        wdtT=np.ascontiguousarray(np.asarray(inputs["L0_attn_W_dt"]).T),
        woutT=np.ascontiguousarray(np.asarray(inputs["L0_attn_W_out"]).T),
        w1T=np.ascontiguousarray(np.asarray(inputs["L0_ffn_w1"]).T),
        w2T=np.ascontiguousarray(np.asarray(inputs["L0_ffn_w2"]).T),
        routT=np.ascontiguousarray(np.asarray(inputs["L0_ffn_router_w"]).T),
        rout_b=np.ascontiguousarray(np.asarray(inputs["L0_ffn_router_b"]).reshape(1, NEXP)),
        conv_w=np.ascontiguousarray(np.asarray(inputs["L0_attn_conv_w"])),
        conv_b=np.ascontiguousarray(np.asarray(inputs["L0_attn_conv_b"]).reshape(DI, 1)),
        b_dt=np.ascontiguousarray(np.asarray(inputs["L0_attn_b_dt"]).reshape(DI, 1)),
        a_log=np.ascontiguousarray(np.asarray(inputs["L0_attn_A_log"])),
        bp=np.ascontiguousarray(np.asarray(inputs["L0_attn_Bp"])),
        cp=np.ascontiguousarray(np.asarray(inputs["L0_attn_Cp"])),
        dp=np.ascontiguousarray(np.asarray(inputs["L0_attn_Dp"]).reshape(DI, 1)),
        b1c=np.ascontiguousarray(np.asarray(inputs["L0_ffn_b1"]).reshape(HID // 128, 128).T),
        b2c=np.ascontiguousarray(np.asarray(inputs["L0_ffn_b2"]).reshape(DM // 128, 128).T),
    )
    in_maps = []
    for k in range(NCORE):
        lo = k * TOK
        if lo >= HALO:
            xs = np.ascontiguousarray(xT[:, lo - HALO: lo + TOK])
        else:
            xs = np.ascontiguousarray(
                np.concatenate([np.zeros((DM, HALO), np.float32), xT[:, :TOK]], axis=1))
        im = {"x": xs}
        im.update(shared)
        in_maps.append(im)

    nc = build_core()
    res = run_bass_kernel_spmd(nc, in_maps, core_ids=list(range(NCORE)))
    outs = [np.asarray(res.results[k]["out"]).T for k in range(NCORE)]
    return np.concatenate(outs, axis=0).reshape(1, SEQ, DM).astype(np.float32)
